# version 12
# baseline (speedup 1.0000x reference)
import numpy as np
from contextlib import ExitStack
import concourse.bass as bass
import concourse.mybir as mybir
from concourse.bass_utils import run_bass_kernel_spmd

F32 = mybir.dt.float32
BF16 = mybir.dt.bfloat16
AF = mybir.ActivationFunctionType
ALU = mybir.AluOpType
AXX = mybir.AxisListType.X

NCORES = 8
D = 1024
TOK = 2048
NT = 16
H = 4
DK = 256
DV = 512
CH = 256
NCH = TOK // CH
EPS = 1e-5
KC = 8

ENGS = ("pe", "dv", "ac", "gp", "sy")


class Tape:

    def __init__(self):
        self.rows = []
        self.cnt = {}
        self.pend = {e: [] for e in ENGS}

    def cur(self, sem):
        return self.cnt.get(sem, 0)

    def emit(self, eng, fn, waits=(), incs=()):
        w = self.pend[eng] + list(waits)
        self.pend[eng] = []
        if eng in ("dv", "ac", "gp") and self.cur(eng) > 0:
            w.append((eng, self.cur(eng)))
        for s, v in w:
            assert v <= self.cur(s), f"wait {s}>={v} not satisfiable (cur {self.cur(s)})"
        self.rows.append((eng, fn, w, list(incs)))
        for s, n in incs:
            self.cnt[s] = self.cur(s) + n
        return [(s, self.cur(s)) for s, _ in incs]

    def barrier(self):
        dma_waits = {"sy": [], "gp": []}
        for s, v in self.cnt.items():
            if s.startswith("syd") and v > 0:
                dma_waits["sy"].append((s, v))
            elif s.startswith("gpd") and v > 0:
                dma_waits["gp"].append((s, v))
        for e in ENGS:
            self.emit(e, lambda en: en.nop(), waits=dma_waits.get(e, ()),
                      incs=(("bar", 1),))
        v = self.cur("bar")
        for e in ENGS:
            self.pend[e].append(("bar", v))

    def realize(self, block, semh):
        for eng_name, deco in (("pe", block.tensor), ("dv", block.vector),
                               ("ac", block.scalar), ("gp", block.gpsimd),
                               ("sy", block.sync)):
            def mk(eng_name=eng_name):
                def emitter(e):
                    seen = {}
                    for g, fn, waits, incs in self.rows:
                        if g != eng_name:
                            continue
                        for s, v in waits:
                            if seen.get(s, 0) < v:
                                e.wait_ge(semh[s], v)
                                seen[s] = v
                        inst = fn(e)
                        for s, n in incs:
                            inst.then_inc(semh[s], n)
                return emitter
            deco(mk())


def host_constants():
    c = {}
    n = np.arange(D)
    ang = 2.0 * np.pi * np.outer(n, n) / D
    c["C"] = np.cos(ang)
    c["S"] = -np.sin(ang)
    inv_freq = 1.0 / (10000.0 ** (np.arange(0, DK, 2) / DK))
    t = np.arange(4096)
    fr = np.outer(t, inv_freq)
    c["cos"] = np.cos(fr)
    c["sin"] = np.sin(fr)
    gamma = 1.0 - np.exp2(-5.0 - np.arange(H))
    c["gamma"] = gamma
    idx = np.arange(CH)
    dts = np.zeros((128, H * 2 * CH))
    for h in range(H):
        g = gamma[h]
        DT = np.where(idx[:, None] <= idx[None, :],
                      g ** (idx[None, :] - idx[:, None]), 0.0) / 16.0
        for jt in range(2):
            dts[:, (h * 2 + jt) * CH:(h * 2 + jt + 1) * CH] = DT[jt * 128:(jt + 1) * 128, :]
    c["dts"] = dts
    cqb = np.zeros((128, H * CH))
    ckb = np.zeros((128, H * CH))
    for h in range(H):
        g = gamma[h]
        cqb[:, h * CH:(h + 1) * CH] = (g ** (idx + 1.0) / 16.0)[None, :]
        ckb[:, h * CH:(h + 1) * CH] = (g ** (CH - 1.0 - idx))[None, :]
    c["cqb"] = cqb
    c["ckb"] = ckb
    c["ident"] = np.eye(128)
    return c


def build_nc(debug=False):
    nc = bass.Bass("TRN2", target_bir_lowering=False)
    dbg_kind = "ExternalOutput" if debug else "Internal"

    dx = nc.dram_tensor("x", [TOK, D], F32, kind="ExternalInput")
    dwq = nc.dram_tensor("wq", [D, D], F32, kind="ExternalInput")
    dwk = nc.dram_tensor("wk", [D, D], F32, kind="ExternalInput")
    dwv = nc.dram_tensor("wv", [D, 2048], F32, kind="ExternalInput")
    dwg = nc.dram_tensor("wg", [D, 2048], F32, kind="ExternalInput")
    dwo = nc.dram_tensor("wo", [2048, D], F32, kind="ExternalInput")
    dwfr = nc.dram_tensor("wfr", [D, D], F32, kind="ExternalInput")
    dwfi = nc.dram_tensor("wfi", [D, D], F32, kind="ExternalInput")
    dbias = nc.dram_tensor("bias", [1, D], F32, kind="ExternalInput")
    dgnwt = nc.dram_tensor("gnwt", [128, 16], F32, kind="ExternalInput")
    dfrs = nc.dram_tensor("frs", [D, 128], F32, kind="ExternalInput")
    dfis = nc.dram_tensor("fis", [D, 128], F32, kind="ExternalInput")
    dfisn = nc.dram_tensor("fisn", [D, 128], F32, kind="ExternalInput")
    dcos = nc.dram_tensor("cosf", [128, TOK], F32, kind="ExternalInput")
    dsin = nc.dram_tensor("sinf", [128, TOK], F32, kind="ExternalInput")
    ddts = nc.dram_tensor("dts", [128, H * 2 * CH], F32, kind="ExternalInput")
    dcqb = nc.dram_tensor("cqb", [128, H * CH], F32, kind="ExternalInput")
    dckb = nc.dram_tensor("ckb", [128, H * CH], F32, kind="ExternalInput")
    dident = nc.dram_tensor("identc", [128, 128], F32, kind="ExternalInput")
    dflag = nc.dram_tensor("flagcol", [128, 1], F32, kind="ExternalInput")
    dy = nc.dram_tensor("y", [TOK, D], F32, kind="ExternalOutput")

    o_hbm = nc.dram_tensor("o_hbm", [TOK, 2048], BF16, kind=dbg_kind)
    qhat_hbm = nc.dram_tensor("qhat_hbm", [D, TOK], BF16, kind=dbg_kind)
    cc1_in = nc.dram_tensor("cc1_in", [4, D, 128], BF16)
    cc1_out = nc.dram_tensor("cc1_out", [NCORES * 4, D, 128], BF16, addr_space="Shared")
    cc2_in = nc.dram_tensor("cc2_in", [2, D, 128], BF16)
    cc2_out = nc.dram_tensor("cc2_out", [NCORES * 2, D, 128], BF16, addr_space="Shared")
    cc3_in = nc.dram_tensor("cc3_in", [H * 2 * 128, DV], F32, kind=dbg_kind)
    dbg_q = nc.dram_tensor("dbg_q", [128, 4096], BF16, kind=dbg_kind)
    cc3_out = nc.dram_tensor("cc3_out", [2 * H * 2 * 128, DV], F32)

    gamma = 1.0 - np.exp2(-5.0 - np.arange(H))
    gC = [float(gamma[h] ** CH) for h in range(H)]
    ghat = [[float(gamma[h] ** (CH * cc)) for cc in range(NCH)] for h in range(H)]

    T = Tape()
    stk = ExitStack()
    bg = {}

    def bgw(*keys):
        return [bg[k] for k in keys if k in bg]

    def sb(name, cols, dtype):
        return stk.enter_context(nc.sbuf_tensor(name, [128, cols], dtype))

    semnames = ([f"syd{i}" for i in range(12)] + [f"gpd{i}" for i in range(6)]
                + ["pe", "dv", "ac", "gp", "cc", "bar"])
    semh = {s: stk.enter_context(nc.semaphore(s)) for s in semnames}

    x_sb = sb("x_sb", NT * D, F32)
    cqb_sb = sb("cqb_sb", H * CH, BF16)
    ckb_sb = sb("ckb_sb", H * CH, BF16)
    ident_bf = sb("ident_bf", 128, BF16)
    gnwt_sb = sb("gnwt_sb", 16, F32)
    flag_sb = sb("flag_sb", 1, F32)
    ones_row = sb("ones_row", 128, BF16)
    junk = sb("junk", D, BF16)
    stats = sb("stats", 64, F32)
    r_ln = sb("r_ln", 3 * NT, F32)
    nmr_ln = sb("nmr_ln", 3 * NT, F32)
    ssq = sb("ssq", 8, F32)
    rh_sb = sb("rh_sb", 4, F32)

    ps = [stk.enter_context(nc.psum_tensor(f"ps{i}", [128, 512], F32)) for i in range(6)]
    pt0 = stk.enter_context(nc.psum_tensor("pt0", [128, 1024], BF16))
    pt1 = stk.enter_context(nc.psum_tensor("pt1", [128, 1024], BF16))

    _dma_pool = {"sy": [f"syd{i}" for i in range(12)],
                 "gp": [f"gpd{i}" for i in range(6)]}
    _dma_rr = {"sy": 0, "gp": 0}

    def dma(_legacy, eng, out, in_, waits=()):
        pool = _dma_pool[eng]
        s = pool[_dma_rr[eng] % len(pool)]
        _dma_rr[eng] += 1
        prev = T.cur(s)
        return T.emit(eng, lambda e, o=out, i=in_: e.dma_start(out=o, in_=i),
                      waits=list(waits) + [(s, prev)], incs=((s, 16),))[0]

    w_x = dma("ld", "sy", x_sb[:, :],
              bass.AP(dx, 0, [[D, 128], [128 * D, NT], [1, D]]))
    w_cqb = dma("gld", "gp", cqb_sb[:, :], dcqb[:, :])
    w_ckb = dma("gld", "gp", ckb_sb[:, :], dckb[:, :])
    w_id = dma("gld", "gp", ident_bf[:, :], dident[:, :])
    w_gnw = dma("ld", "sy", gnwt_sb[:, :], dgnwt[:, :])
    w_flag = dma("ld", "sy", flag_sb[:, :], dflag[:, :])
    T.emit("dv", lambda e: e.memset(ones_row[0:1, :], 0.5), incs=(("dv", 1),))
    ones_tok = ("dv", T.cur("dv"))

    def ln_stats(tile, waits_dv=(), waits_ac=None, scratch=None, eng="dv"):
        scr = junk if scratch is None else scratch
        T.emit(eng, lambda e, t=tile: e.reduce_sum(
            stats[:, t:t + 1], x_sb[:, t * D:(t + 1) * D], axis=AXX),
            waits=waits_dv, incs=((eng, 1),))
        T.emit(eng, lambda e, t=tile, scr=scr: e.tensor_tensor(
            scr[:, 0:D], x_sb[:, t * D:(t + 1) * D], x_sb[:, t * D:(t + 1) * D], ALU.mult),
            incs=((eng, 1),))
        T.emit(eng, lambda e, t=tile, scr=scr: e.reduce_sum(
            stats[:, 16 + t:17 + t], scr[:, 0:D], axis=AXX),
            incs=((eng, 1),))

    def rsqrt_chain(slot, nfeat, waits=()):
        a = stats[:, 32:48]
        b = stats[:, 48:64]
        sl = slice(slot * NT, slot * NT + NT)
        inv = 1.0 / nfeat
        T.emit("dv", lambda e: e.tensor_scalar_mul(a, stats[:, 0:16], inv),
               waits=waits, incs=(("dv", 1),))
        T.emit("dv", lambda e: e.tensor_scalar_mul(b, stats[:, 16:32], inv), incs=(("dv", 1),))
        T.emit("dv", lambda e: e.tensor_tensor(stats[:, 0:16], a, a, ALU.mult), incs=(("dv", 1),))
        T.emit("dv", lambda e: e.tensor_tensor(b, b, stats[:, 0:16], ALU.subtract), incs=(("dv", 1),))
        T.emit("dv", lambda e: e.tensor_scalar_add(b, b, EPS), incs=(("dv", 1),))
        T.emit("dv", lambda e: e.reciprocal(stats[:, 0:16], b), incs=(("dv", 1),))
        tok_dv = ("dv", T.cur("dv"))
        T.emit("ac", lambda e: e.activation(r_ln[:, sl], stats[:, 0:16], AF.Sqrt,
                                            bias=0.0, scale=1.0),
               waits=[tok_dv], incs=(("ac", 1),))
        tok_ac = ("ac", T.cur("ac"))
        T.emit("dv", lambda e: e.tensor_tensor(stats[:, 0:16], r_ln[:, sl], r_ln[:, sl], ALU.mult),
               waits=[tok_ac], incs=(("dv", 1),))
        T.emit("dv", lambda e: e.tensor_tensor(stats[:, 0:16], stats[:, 0:16], b, ALU.mult),
               incs=(("dv", 1),))
        T.emit("dv", lambda e: e.tensor_scalar(stats[:, 0:16], stats[:, 0:16], -0.5, 1.5,
                                               ALU.mult, ALU.add), incs=(("dv", 1),))
        T.emit("dv", lambda e: e.tensor_tensor(r_ln[:, sl], r_ln[:, sl], stats[:, 0:16], ALU.mult),
               incs=(("dv", 1),))
        T.emit("dv", lambda e: e.tensor_tensor(stats[:, 0:16], a, r_ln[:, sl], ALU.mult),
               incs=(("dv", 1),))
        T.emit("dv", lambda e: e.tensor_scalar_mul(nmr_ln[:, sl], stats[:, 0:16], -1.0),
               incs=(("dv", 1),))
        return ("dv", T.cur("dv"))

    for t in range(NT):
        ln_stats(t, waits_dv=[w_x], waits_ac=[w_x])
    ln1_stats = [("dv", T.cur("dv")), ("ac", T.cur("ac"))]

    with ExitStack() as form_stk:
        def fsb(name, cols, dtype=BF16):
            return form_stk.enter_context(nc.sbuf_tensor(name, [128, cols], dtype))
        frsh = fsb("frsh", 1024); frsl = fsb("frsl", 1024)
        fish = fsb("fish", 1024); fisl = fsb("fisl", 1024)
        fisnh = fsb("fisnh", 1024); fisnl = fsb("fisnl", 1024)
        ath = fsb("ath", 1024); atl = fsb("atl", 1024)
        bth = fsb("bth", 1024); btl = fsb("btl", 1024)
        mhb = fsb("mhb", 1024); mlb = fsb("mlb", 1024)
        stg2 = fsb("stg2", 1024, F32)

        def slice_ap(dt_):
            return bass.AP(dt_, 0, [[128, 128], [128 * 128, KC], [1, 128]])
        for drm, hh, ll in ((dfrs, frsh, frsl), (dfis, fish, fisl), (dfisn, fisnh, fisnl)):
            wtok = dma("gld", "gp", hh[:, :], slice_ap(drm))
            wtok2 = dma("ld", "sy", stg2[:, :], slice_ap(drm))
            dv_tok = T.emit("dv", lambda e, hh=hh, ll=ll: e.tensor_tensor(
                ll[:, :], stg2[:, :], hh[:, :], ALU.subtract),
                waits=[wtok, wtok2], incs=(("dv", 1),))[0]
            T.pend["sy"].append(dv_tok)
        fsplit_done = [("dv", T.cur("dv"))]

        with ExitStack() as wsplit_stk:
            def wsb(name):
                return wsplit_stk.enter_context(nc.sbuf_tensor(name, [128, KC * D], BF16))
            wrh, wrl = wsb("wrh"), wsb("wrl")
            wih, wil = wsb("wih"), wsb("wil")
            stgc = wsplit_stk.enter_context(nc.sbuf_tensor("stgc", [128, D], F32))
            for dw, whh, wll in ((dwfr, wrh, wrl), (dwfi, wih, wil)):
                wh_tok = dma("gld", "gp", whh[:, :],
                             bass.AP(dw, 0, [[D, 128], [128 * D, KC], [1, D]]))
                for kc in range(KC):
                    s_tok = dma("ld", "sy", stgc[:, :],
                                bass.AP(dw, kc * 128 * D, [[D, 128], [1, D]]))
                    dv_tok = T.emit("dv", lambda e, kc=kc, wll=wll, whh=whh: e.tensor_tensor(
                        wll[:, kc * D:(kc + 1) * D], stgc[:, :],
                        whh[:, kc * D:(kc + 1) * D], ALU.subtract),
                        waits=[s_tok, wh_tok], incs=(("dv", 1),))[0]
                    T.pend["sy"].append(dv_tok)
            splits_done = fsplit_done + [("dv", T.cur("dv"))]

            def mform_stage(dst_h, dst_l, opnds, scale=None):
                for mt in range(KC):
                    bank = ps[mt % 4]
                    bkey = f"ps{mt % 4}"
                    def fn(e, mt=mt, bank=bank, opnds=tuple(opnds)):
                        n = len(opnds) * KC
                        i = 0
                        last = None
                        for wH, cH in opnds:
                            for kc in range(KC):
                                last = e.matmul(bank[:, 0:128],
                                         wH[:, kc * D + mt * 128: kc * D + mt * 128 + 128],
                                         cH[:, kc * 128:(kc + 1) * 128],
                                         start=(i == 0), stop=(i == n - 1))
                                i += 1
                        return last
                    pe_tok = T.emit("pe", fn, waits=list(splits_done) + bgw(bkey),
                                    incs=(("pe", 1),))[0]
                    if scale is None:
                        T.emit("dv", lambda e, bank=bank, mt=mt, dst_h=dst_h: e.tensor_copy(
                            dst_h[:, mt * 128:(mt + 1) * 128], bank[:, 0:128]),
                            waits=[pe_tok], incs=(("dv", 1),))
                        d2 = T.emit("dv", lambda e, bank=bank, mt=mt, dst_h=dst_h, dst_l=dst_l:
                                    e.scalar_tensor_tensor(
                                        dst_l[:, mt * 128:(mt + 1) * 128], bank[:, 0:128], 1.0,
                                        dst_h[:, mt * 128:(mt + 1) * 128], ALU.mult, ALU.subtract),
                                    incs=(("dv", 1),))[0]
                    else:
                        T.emit("dv", lambda e, bank=bank, mt=mt, dst_h=dst_h: e.tensor_scalar_mul(
                            dst_h[:, mt * 128:(mt + 1) * 128], bank[:, 0:128], scale),
                            waits=[pe_tok], incs=(("dv", 1),))
                        d2 = T.emit("dv", lambda e, bank=bank, mt=mt, dst_h=dst_h, dst_l=dst_l:
                                    e.scalar_tensor_tensor(
                                        dst_l[:, mt * 128:(mt + 1) * 128], bank[:, 0:128], scale,
                                        dst_h[:, mt * 128:(mt + 1) * 128], ALU.mult, ALU.subtract),
                                    incs=(("dv", 1),))[0]
                    bg[bkey] = d2

            mform_stage(ath, atl, [(wrh, frsh), (wrl, frsh), (wrh, frsl),
                                   (wih, fisnh), (wil, fisnh), (wih, fisnl)])
            mform_stage(bth, btl, [(wih, frsh), (wil, frsh), (wih, frsl),
                                   (wrh, fish), (wrl, fish), (wrh, fisl)])
            s1_done = [("dv", T.cur("dv"))]
            ag1_st_toks = [dma("st", "sy",
                    bass.AP(cc1_in, i * D * 128, [[128, 128], [128 * 128, KC], [1, 128]]),
                    srct[:, :], waits=s1_done if i == 0 else ())
                for i, srct in enumerate((ath, atl, bth, btl))]
            T.emit("gp", lambda e: e.collective_compute(
                "AllGather", ALU.bypass, replica_groups=[list(range(NCORES))],
                ins=[cc1_in.ap().opt()], outs=[cc1_out.ap().opt()]),
                waits=ag1_st_toks, incs=(("cc", 1),))
            ag1_done = ("cc", T.cur("cc"))
            T.barrier()

        atfh = fsb("atfh", KC * D); atfl = fsb("atfl", KC * D)
        btfh = fsb("btfh", KC * D); btfl = fsb("btfl", KC * D)
        agload = []
        for ti, dst in enumerate((atfh, atfl, btfh, btfl)):
            for r in range(NCORES):
                agload.append(dma("ld", "sy",
                    bass.AP(dst, 128 * r, [[KC * D, 128], [D, KC], [1, 128]]),
                    bass.AP(cc1_out, (r * 4 + ti) * D * 128,
                            [[128, 128], [128 * 128, KC], [1, 128]]),
                    waits=[ag1_done]))

        mscale = 0.5 / D
        saved = T.pend["pe"]
        T.pend["pe"] = saved + list(agload)
        mform_stage(mhb, mlb, [(atfh, frsh), (atfl, frsh), (atfh, frsl),
                               (btfh, fish), (btfl, fish), (btfh, fisl)], scale=mscale)
        s2_done = [("dv", T.cur("dv"))]
        ag2_st_toks = [dma("st", "sy",
                bass.AP(cc2_in, i * D * 128, [[128, 128], [128 * 128, KC], [1, 128]]),
                srct[:, :], waits=s2_done if i == 0 else ())
            for i, srct in enumerate((mhb, mlb))]
        T.emit("gp", lambda e: e.collective_compute(
            "AllGather", ALU.bypass, replica_groups=[list(range(NCORES))],
            ins=[cc2_in.ap().opt()], outs=[cc2_out.ap().opt()]),
            waits=ag2_st_toks, incs=(("cc", 1),))
        ag2_done = ("cc", T.cur("cc"))
        T.barrier()

    ln1_r = rsqrt_chain(0, D, waits=ln1_stats)

    out_toks = []

    def ffn_pass(slot, msbh, msbl, m_ld, xln_tm, xlnT, ln_tok, do_stats, do_out,
                 brow, brow_tok):
        sl0 = slot * NT
        prep_toks = {}
        prep_pe = {}

        def prep(t):
            par = t % 2
            guard = [prep_pe[t - 2]] if (t - 2) in prep_pe else []
            a_tok = T.emit("ac", lambda e, t=t, par=par: e.activation(
                xln_tm[par][:, :], x_sb[:, t * D:(t + 1) * D], AF.Identity,
                bias=nmr_ln[:, sl0 + t:sl0 + t + 1], scale=r_ln[:, sl0 + t:sl0 + t + 1]),
                waits=[ln_tok] + guard, incs=(("ac", 1),))[0]
            pe_tok = T.emit("pe", lambda e, par=par: [
                e.transpose(pt0[:, kc * 128:(kc + 1) * 128],
                            xln_tm[par][:, kc * 128:(kc + 1) * 128], ident_bf[:, :])
                for kc in range(KC)][-1],
                waits=[a_tok, w_id] + bgw("pt0"), incs=(("pe", 1),))[0]
            dv_tok = T.emit("dv", lambda e, par=par: e.tensor_copy(xlnT[par][:, :], pt0[:, :]),
                            waits=[pe_tok], incs=(("dv", 1),))[0]
            bg["pt0"] = dv_tok
            prep_toks[t] = dv_tok
            prep_pe[t] = pe_tok

        prep(0)
        for t in range(NT):
            if t + 1 < NT:
                prep(t + 1)
            par = t % 2
            dv_tok = prep_toks.pop(t)
            ev_toks = []
            for jb in range(2):
                bank = ps[(t % 2) * 2 + jb]
                bkey = f"ps{(t % 2) * 2 + jb}"
                def fn(e, jb=jb, bank=bank, par=par):
                    e.matmul(bank[:, :], ones_row[0:1, :],
                             brow[0:1, jb * 512:(jb + 1) * 512], start=True, stop=False)
                    last = None
                    for i, msb in enumerate((msbh, msbl)):
                        for kc in range(KC):
                            last = e.matmul(bank[:, :], xlnT[par][:, kc * 128:(kc + 1) * 128],
                                     msb[:, kc * D + jb * 512: kc * D + (jb + 1) * 512],
                                     start=False, stop=(i == 1 and kc == KC - 1))
                    return last
                pe2 = T.emit("pe", fn,
                             waits=[dv_tok, brow_tok, ones_tok] + list(m_ld) + bgw(bkey),
                             incs=(("pe", 1),))[0]
                ev = T.emit("dv", lambda e, t=t, jb=jb, bank=bank: e.scalar_tensor_tensor(
                    x_sb[:, t * D + jb * 512: t * D + (jb + 1) * 512], bank[:, :], 1.0,
                    x_sb[:, t * D + jb * 512: t * D + (jb + 1) * 512], ALU.mult, ALU.add),
                    waits=[pe2], incs=(("dv", 1),))[0]
                ev_toks.append(ev)
                bg[bkey] = ev
            if do_stats:
                ln_stats(t)
            if do_out:
                out_toks.append(dma("st", "sy",
                    bass.AP(dy, t * 128 * D, [[D, 128], [1, D]]),
                    x_sb[:, t * D:(t + 1) * D], waits=ev_toks))

    with ExitStack() as wqkv_stk:
        def qsb(name, cols):
            return wqkv_stk.enter_context(nc.sbuf_tensor(name, [128, cols], BF16))
        Wq_sb = qsb("Wq_sb", KC * D)
        Wk_sb = qsb("Wk_sb", KC * D)
        Wv_sb = qsb("Wv_sb", KC * 2048)
        _wts = [dma("gld", "gp", dst[:, :],
                    bass.AP(srct, 0, [[width, 128], [128 * width, KC], [1, width]]))
                for dst, srct, width in ((Wq_sb, dwq, D), (Wk_sb, dwk, D), (Wv_sb, dwv, 2048))]
        wqkv_loaded = _wts

        with ExitStack() as ffn1_stk:
            Mh_sb = ffn1_stk.enter_context(nc.sbuf_tensor("Mh_sb", [128, KC * D], BF16))
            Ml_sb = ffn1_stk.enter_context(nc.sbuf_tensor("Ml_sb", [128, KC * D], BF16))
            xln_tm1 = [ffn1_stk.enter_context(nc.sbuf_tensor(f"xln_tm1{p}", [128, D], BF16))
                       for p in range(2)]
            xlnT1 = [ffn1_stk.enter_context(nc.sbuf_tensor(f"xlnT1{p}", [128, D], BF16))
                     for p in range(2)]
            brow1 = ffn1_stk.enter_context(nc.sbuf_tensor("brow1", [128, D], BF16))
            brow1_tok = dma("gld", "gp", brow1[0:1, :], dbias[:, :])
            m_ld = []
            for i, dst in enumerate((Mh_sb, Ml_sb)):
                for r in range(NCORES):
                    m_ld.append(dma("ld", "sy",
                        bass.AP(dst, 128 * r, [[KC * D, 128], [D, KC], [1, 128]]),
                        bass.AP(cc2_out, (r * 2 + i) * D * 128,
                                [[128, 128], [128 * 128, KC], [1, 128]]),
                        waits=[ag2_done]))
            ffn_pass(0, Mh_sb, Ml_sb, m_ld, xln_tm1, xlnT1, ln1_r, do_stats=True,
                     do_out=False, brow=brow1, brow_tok=brow1_tok)
            T.barrier()
        ln2_r = rsqrt_chain(1, D)

        with ExitStack() as pa_stk:
            def asb(name, cols, dtype=BF16):
                return pa_stk.enter_context(nc.sbuf_tensor(name, [128, cols], dtype))
            S32 = asb("S32", H * 2 * DV, F32)
            dts_sb = asb("dts_sb", H * 2 * CH, F32)
            w_dts = dma("ld", "sy", dts_sb[:, :], ddts[:, :])
            S_bf = asb("S_bf", H * 2 * DV)
            xln2_tm = [asb(f"xln2_tm{p}", D) for p in range(2)]
            xlnT2 = asb("xlnT2", KC * CH)
            qh_raw = [asb(f"qh_raw{p}", 512) for p in range(2)]
            kh_raw = [asb(f"kh_raw{p}", 512) for p in range(2)]
            qh_fm = [asb(f"qh_fm{p}", 512) for p in range(2)]
            qt_fm = [asb(f"qt_fm{p}", 512) for p in range(2)]
            kh_fm = [asb(f"kh_fm{p}", 512) for p in range(2)]
            khat_tm = [asb(f"khat_tm{p}", 512) for p in range(2)]
            vh_tm = [asb(f"vh_tm{p}", 2 * DV) for p in range(2)]
            attnT_sb = [asb(f"attnT_sb{p}", 2 * 256) for p in range(2)]
            o_stage = asb("o_stage", 2 * 2048)
            qhat_stage = asb("qhat_stage", 2048)
            cos_c = [asb(f"cos_c{p}", CH) for p in range(2)]
            sin_c = [asb(f"sin_c{p}", CH) for p in range(2)]

            T.emit("dv", lambda e: e.memset(S32[:, :], 0.0), incs=(("dv", 1),))
            T.emit("dv", lambda e: e.memset(S_bf[:, :], 0.0), incs=(("dv", 1),))
            sinit = [("dv", T.cur("dv"))]

            prev_rot_q = [None, None]
            prev_rot_k = [None, None]
            part1_toks = {}

            def part1(c, h, xlnT2_ready):
                par = h % 2
                def qk_proj(W, bank, h=h):
                    def fn(e, W=W, bank=bank, h=h):
                        last = None
                        for dc in range(2):
                            for kc in range(KC):
                                last = e.matmul(bank[:, dc * 256:(dc + 1) * 256],
                                         W[:, kc * D + h * 256 + dc * 128:
                                           kc * D + h * 256 + dc * 128 + 128],
                                         xlnT2[:, kc * CH:(kc + 1) * CH],
                                         start=(kc == 0), stop=(kc == KC - 1))
                        return last
                    return fn
                qpe = T.emit("pe", qk_proj(Wq_sb, ps[0]),
                             waits=[xlnT2_ready] + wqkv_loaded + bgw("ps0"),
                             incs=(("pe", 1),))[0]
                qev = T.emit("ac", lambda e, par=par: e.copy(qh_raw[par][:, :], ps[0][:, :]),
                             waits=[qpe] + ([prev_rot_q[par]] if prev_rot_q[par] else []),
                             incs=(("ac", 1),))[0]
                bg["ps0"] = qev
                kpe = T.emit("pe", qk_proj(Wk_sb, ps[1]), waits=bgw("ps1"),
                             incs=(("pe", 1),))[0]
                kev = T.emit("ac", lambda e, par=par: e.copy(kh_raw[par][:, :], ps[1][:, :]),
                             waits=[kpe] + ([prev_rot_k[par]] if prev_rot_k[par] else []),
                             incs=(("ac", 1),))[0]
                bg["ps1"] = kev
                vev_toks = []
                for jt in range(2):
                    bank = ps[2 + jt]
                    def fnv(e, jt=jt, bank=bank, h=h):
                        last = None
                        for kc in range(KC):
                            last = e.matmul(bank[:, :],
                                     xlnT2[:, kc * CH + jt * 128: kc * CH + jt * 128 + 128],
                                     Wv_sb[:, kc * 2048 + h * DV: kc * 2048 + (h + 1) * DV],
                                     start=(kc == 0), stop=(kc == KC - 1))
                        return last
                    vpe = T.emit("pe", fnv, waits=bgw(f"ps{2 + jt}"),
                                 incs=(("pe", 1),))[0]
                    vev = T.emit("ac", lambda e, jt=jt, bank=bank, par=par: e.copy(
                        vh_tm[par][:, jt * DV:(jt + 1) * DV], bank[:, :]),
                        waits=[vpe], incs=(("ac", 1),))[0]
                    vev_toks.append(vev)
                    bg[f"ps{2 + jt}"] = vev
                part1_toks[(c, h)] = (qev, kev, vev_toks)

            def part2(c, h):
                par = h % 2
                qev, kev, vev_toks = part1_toks.pop((c, h))
                cosl = cos_c[c % 2][:, :]
                sinl = sin_c[c % 2][:, :]
                w_cos, w_sin = rot_toks[c]

                def rot(dst, srcb, wait_tok, eng, so, cosl=cosl, sinl=sinl):
                    T.emit(eng, lambda e, dst=dst, srcb=srcb, cosl=cosl, so=so: e.tensor_tensor(
                        junk[:, so:so + 256], srcb[:, 0:256], cosl, ALU.mult),
                        waits=[wait_tok, w_cos, w_sin], incs=((eng, 1),))
                    T.emit(eng, lambda e, srcb=srcb, sinl=sinl, so=so: e.tensor_tensor(
                        junk[:, so + 256:so + 512], srcb[:, 256:512], sinl, ALU.mult),
                        incs=((eng, 1),))
                    T.emit(eng, lambda e, dst=dst, so=so: e.tensor_tensor(
                        dst[:, 0:256], junk[:, so:so + 256], junk[:, so + 256:so + 512],
                        ALU.subtract),
                        incs=((eng, 1),))
                    T.emit(eng, lambda e, srcb=srcb, cosl=cosl, so=so: e.tensor_tensor(
                        junk[:, so:so + 256], srcb[:, 256:512], cosl, ALU.mult),
                        incs=((eng, 1),))
                    T.emit(eng, lambda e, srcb=srcb, sinl=sinl, so=so: e.tensor_tensor(
                        junk[:, so + 256:so + 512], srcb[:, 0:256], sinl, ALU.mult),
                        incs=((eng, 1),))
                    return T.emit(eng, lambda e, dst=dst, so=so: e.tensor_tensor(
                        dst[:, 256:512], junk[:, so:so + 256], junk[:, so + 256:so + 512],
                        ALU.add),
                        incs=((eng, 1),))[0]

                q_rot = rot(qh_fm[par], qh_raw[par], qev, "dv", 0)
                prev_rot_q[par] = q_rot
                for dc in range(2):
                    T.emit("dv", lambda e, dc=dc, h=h, par=par: e.tensor_tensor(
                        qt_fm[par][:, dc * 256:(dc + 1) * 256],
                        qh_fm[par][:, dc * 256:(dc + 1) * 256],
                        cqb_sb[:, h * CH:(h + 1) * CH], ALU.mult),
                        waits=[w_cqb], incs=(("dv", 1),))
                    T.emit("dv", lambda e, dc=dc, h=h, c=c, par=par: e.tensor_scalar_mul(
                        qhat_stage[:, (2 * h + dc) * 256:(2 * h + dc + 1) * 256],
                        qt_fm[par][:, dc * 256:(dc + 1) * 256], ghat[h][c]),
                        incs=(("dv", 1),))
                qt_done = ("dv", T.cur("dv"))
                k_rot = rot(kh_fm[par], kh_raw[par], kev, "gp", 512)
                for dc in range(2):
                    T.emit("gp", lambda e, dc=dc, h=h, par=par: e.tensor_tensor(
                        kh_raw[par][:, dc * 256:(dc + 1) * 256],
                        kh_fm[par][:, dc * 256:(dc + 1) * 256],
                        ckb_sb[:, h * CH:(h + 1) * CH], ALU.mult),
                        waits=[w_ckb], incs=(("gp", 1),))
                khat_fm_done = ("gp", T.cur("gp"))
                prev_rot_k[par] = khat_fm_done

                kt_pe = T.emit("pe", lambda e, par=par: [
                    e.transpose(pt1[:, (jt * 2 + dc) * 128:(jt * 2 + dc + 1) * 128],
                                kh_raw[par][:, dc * 256 + jt * 128: dc * 256 + jt * 128 + 128],
                                ident_bf[:, :])
                    for jt in range(2) for dc in range(2)][-1],
                    waits=[khat_fm_done, w_id] + bgw("pt1"), incs=(("pe", 1),))[0]
                kt_ev = T.emit("dv", lambda e, par=par: e.tensor_copy(
                    khat_tm[par][:, :], pt1[:, 0:512]),
                    waits=[kt_pe], incs=(("dv", 1),))[0]
                bg["pt1"] = kt_ev

                at_pe = T.emit("pe", lambda e, par=par: [
                    e.matmul(ps[4][:, jt * 256:(jt + 1) * 256],
                             kh_fm[par][:, dc * 256 + jt * 128: dc * 256 + jt * 128 + 128],
                             qh_fm[par][:, dc * 256:(dc + 1) * 256],
                             start=(dc == 0), stop=(dc == 1))
                    for jt in range(2) for dc in range(2)][-1],
                    waits=[q_rot, k_rot] + bgw("ps4"), incs=(("pe", 1),))[0]
                atm_last = None
                for jt in range(2):
                    atm_last = T.emit("dv", lambda e, jt=jt, h=h, par=par: e.tensor_tensor(
                        attnT_sb[par][:, jt * 256:(jt + 1) * 256],
                        ps[4][:, jt * 256:(jt + 1) * 256],
                        dts_sb[:, (h * 2 + jt) * CH:(h * 2 + jt + 1) * CH], ALU.mult),
                        waits=[at_pe, w_dts], incs=(("dv", 1),))[0]
                bg["ps4"] = atm_last

                o_banks = (ps[5], ps[4])
                for it in range(2):
                    bank = o_banks[it]
                    bk = "ps5" if it == 0 else "ps4"
                    def fno(e, it=it, bank=bank, h=h, par=par):
                        for jt in range(2):
                            e.matmul(bank[:, :],
                                     attnT_sb[par][:, jt * 256 + it * 128: jt * 256 + it * 128 + 128],
                                     vh_tm[par][:, jt * DV:(jt + 1) * DV],
                                     start=(jt == 0), stop=False)
                        last = None
                        for dc in range(2):
                            last = e.matmul(bank[:, :],
                                     qt_fm[par][:, dc * 256 + it * 128: dc * 256 + it * 128 + 128],
                                     S_bf[:, (h * 2 + dc) * DV:(h * 2 + dc + 1) * DV],
                                     start=False, stop=(dc == 1))
                        return last
                    ope = T.emit("pe", fno,
                                 waits=[atm_last, qt_done, vev_toks[-1]] + bgw(bk) +
                                       (sinit if (c == 0 and h == 0) else []),
                                 incs=(("pe", 1),))[0]
                    oev = T.emit("dv", lambda e, it=it, bank=bank, h=h, c=c: e.tensor_copy(
                        o_stage[:, it * 2048 + h * DV: it * 2048 + (h + 1) * DV], bank[:, :]),
                        waits=[ope], incs=(("dv", 1),))[0]
                    bg[bk] = oev

                tu_banks = (ps[5], ps[4])
                tu_pe = T.emit("pe", lambda e, tb=tu_banks, par=par: [
                    e.matmul(tb[dc][:, :],
                             khat_tm[par][:, jt * 256 + dc * 128: jt * 256 + dc * 128 + 128],
                             vh_tm[par][:, jt * DV:(jt + 1) * DV],
                             start=(jt == 0), stop=(jt == 1))
                    for dc in range(2) for jt in range(2)][-1],
                    waits=[kt_ev] + bgw("ps5", "ps4"), incs=(("pe", 1),))[0]
                for dc in range(2):
                    tok_s = T.emit("dv", lambda e, dc=dc, h=h, tb=tu_banks: e.scalar_tensor_tensor(
                        S32[:, (h * 2 + dc) * DV:(h * 2 + dc + 1) * DV],
                        S32[:, (h * 2 + dc) * DV:(h * 2 + dc + 1) * DV], gC[h],
                        tb[dc][:, :], ALU.mult, ALU.add),
                        waits=[tu_pe], incs=(("dv", 1),))[0]
                    bg["ps5" if dc == 0 else "ps4"] = tok_s
                T.emit("dv", lambda e, h=h: e.tensor_copy(
                    S_bf[:, h * 2 * DV:(h + 1) * 2 * DV],
                    S32[:, h * 2 * DV:(h + 1) * 2 * DV]),
                    incs=(("dv", 1),))

            rot_toks = {}

            pre_pe = {}

            def chunk_pre(c):
                rot_war = [("dv", T.cur("dv")), ("gp", T.cur("gp"))]
                rot_toks[c] = (
                    dma("gld", "gp", cos_c[c % 2][:, :],
                        bass.AP(dcos, c * CH, [[TOK, 128], [1, CH]]), waits=rot_war),
                    dma("gld", "gp", sin_c[c % 2][:, :],
                        bass.AP(dsin, c * CH, [[TOK, 128], [1, CH]]), waits=rot_war),
                )
                for jt in range(2):
                    t = 2 * c + jt
                    guard = [pre_pe[(c - 1, jt)]] if (c - 1, jt) in pre_pe else []
                    a_tok = T.emit("ac", lambda e, t=t, jt=jt: e.activation(
                        xln2_tm[jt][:, :], x_sb[:, t * D:(t + 1) * D], AF.Identity,
                        bias=nmr_ln[:, NT + t:NT + t + 1], scale=r_ln[:, NT + t:NT + t + 1]),
                        waits=[ln2_r] + guard, incs=(("ac", 1),))[0]
                    pe_tok = T.emit("pe", lambda e, jt=jt: [
                        e.transpose(pt0[:, kc * 128:(kc + 1) * 128],
                                    xln2_tm[jt][:, kc * 128:(kc + 1) * 128],
                                    ident_bf[:, :]) for kc in range(KC)][-1],
                        waits=[a_tok, w_id] + bgw("pt0"), incs=(("pe", 1),))[0]
                    dv_tok = T.emit("dv", lambda e, jt=jt: e.tensor_copy(
                        bass.AP(xlnT2, jt * 128, [[KC * CH, 128], [CH, KC], [1, 128]]),
                        pt0[:, :]),
                        waits=[pe_tok], incs=(("dv", 1),))[0]
                    bg["pt0"] = dv_tok
                    pre_pe[(c, jt)] = pe_tok
                return ("dv", T.cur("dv"))

            pre_tok = chunk_pre(0)
            part1(0, 0, pre_tok)
            for c in range(NCH):
                for h in range(H):
                    if h + 1 < H:
                        part1(c, h + 1, pre_tok)
                    elif c + 1 < NCH:
                        pre_tok = chunk_pre(c + 1)
                        part1(c + 1, 0, pre_tok)
                    part2(c, h)

                ostage_done = ("dv", T.cur("dv"))
                t1 = dma("st", "sy",
                         bass.AP(o_hbm, c * CH * 2048, [[2048, 128], [128 * 2048, 2], [1, 2048]]),
                         bass.AP(o_stage, 0, [[2 * 2048, 128], [2048, 2], [1, 2048]]),
                         waits=[ostage_done])
                t2 = dma("st", "sy",
                         bass.AP(qhat_hbm, c * CH, [[TOK, 128], [128 * TOK, KC], [1, CH]]),
                         bass.AP(qhat_stage, 0, [[2048, 128], [CH, KC], [1, CH]]))
                T.pend["dv"].append(t1)
                T.pend["dv"].append(t2)
            s_final = ("dv", T.cur("dv"))
            sdma = dma("st", "sy",
                bass.AP(cc3_in, 0, [[DV, 128], [2 * 128 * DV, H], [128 * DV, 2], [1, DV]]),
                bass.AP(S32, 0, [[H * 2 * DV, 128], [2 * DV, H], [DV, 2], [1, DV]]),
                waits=[s_final])
            T.emit("gp", lambda e: e.collective_compute(
                "AllGather", ALU.bypass,
                replica_groups=[[0, 1], [2, 3], [4, 5], [6, 7]],
                ins=[cc3_in.ap().opt()], outs=[cc3_out.ap().opt()]),
                waits=[sdma], incs=(("cc", 1),))
            ag3_done = ("cc", T.cur("cc"))
            T.barrier()

    with ExitStack() as pb_stk:
        def bsb(name, cols, dtype=BF16):
            return pb_stk.enter_context(nc.sbuf_tensor(name, [128, cols], dtype))
        Wg_sb = bsb("Wg_sb", KC * 2048)
        Wo_sb = bsb("Wo_sb", 16 * D)
        S_seed = bsb("S_seed", H * 2 * DV)
        o_in = [bsb(f"o_in{p}", 2048) for p in range(4)]
        qhat_in = [bsb(f"qhat_in{p}", 2048) for p in range(3)]
        o_f32 = bsb("o_f32", 2048, F32)
        g_bf = [bsb(f"g_bf{p}", 2048) for p in range(4)]
        ogT = bsb("ogT", 16 * 128)
        xln_tmB = [bsb(f"xln_tmB{p}", D) for p in range(2)]
        xlnTB = bsb("xlnTB", D)

        _wg_tok = dma("gld", "gp", Wg_sb[:, :],
            bass.AP(dwg, 0, [[2048, 128], [128 * 2048, KC], [1, 2048]]))
        _wo_tok = dma("gld", "gp", Wo_sb[:, :],
            bass.AP(dwo, 0, [[D, 128], [128 * D, 16], [1, D]]))
        wgwo_tok = _wo_tok
        for vc in range(16):
            T.emit("dv", lambda e, vc=vc: e.tensor_scalar_mul(
                Wo_sb[:, vc * D:(vc + 1) * D], Wo_sb[:, vc * D:(vc + 1) * D],
                gnwt_sb[:, vc:vc + 1]),
                waits=[_wg_tok, _wo_tok, w_gnw], incs=(("dv", 1),))
        wo_ready = ("dv", T.cur("dv"))
        sc = dma("gld", "gp", S_seed[:, :],
                 bass.AP(cc3_out, 0, [[DV, 128], [2 * 128 * DV, H], [128 * DV, 2], [1, DV]]),
                 waits=[ag3_done])
        for i in range(4):
            T.emit("dv", lambda e, i=i: e.tensor_scalar_mul(
                S_seed[:, i * 2 * DV:(i + 1) * 2 * DV],
                S_seed[:, i * 2 * DV:(i + 1) * 2 * DV], flag_sb[:, 0:1]),
                waits=[sc, w_flag], incs=(("dv", 1),))
        seed_ready = ("dv", T.cur("dv"))

        b1_toks = {}
        b1_pe = {}
        qh_tok = None

        def phb1(t):
            nonlocal qh_tok
            cchunk = t // 2
            cpar = cchunk % 3
            par = t % 4
            if t % 2 == 0:
                qh_tok = dma("ld", "sy",
                             bass.AP(qhat_in[cpar], 0, [[2048, 128], [CH, KC], [1, CH]]),
                             bass.AP(qhat_hbm, cchunk * CH,
                                     [[TOK, 128], [128 * TOK, KC], [1, CH]]))
            oi_tok = dma("ld", "sy", o_in[par][:, :],
                         bass.AP(o_hbm, t * 128 * 2048, [[2048, 128], [1, 2048]]))
            guard = [b1_pe[t - 2]] if (t - 2) in b1_pe else []
            a_tok = T.emit("ac", lambda e, t=t, bpar=t % 2: e.activation(
                xln_tmB[bpar][:, :], x_sb[:, t * D:(t + 1) * D], AF.Identity,
                bias=nmr_ln[:, NT + t:NT + t + 1], scale=r_ln[:, NT + t:NT + t + 1]),
                waits=guard, incs=(("ac", 1),))[0]
            pe_tok = T.emit("pe", lambda e, bpar=t % 2: [
                e.transpose(pt0[:, kc * 128:(kc + 1) * 128],
                            xln_tmB[bpar][:, kc * 128:(kc + 1) * 128], ident_bf[:, :])
                for kc in range(KC)][-1],
                waits=[a_tok] + bgw("pt0"), incs=(("pe", 1),))[0]
            xT_tok = T.emit("dv", lambda e: e.tensor_copy(xlnTB[:, :], pt0[:, :]),
                            waits=[pe_tok], incs=(("dv", 1),))[0]
            bg["pt0"] = xT_tok
            b1_pe[t] = pe_tok
            g_toks = []
            for jb in range(4):
                bank = ps[2 + jb % 2]
                bk = f"ps{2 + jb % 2}"
                def fng(e, jb=jb, bank=bank):
                    last = None
                    for kc in range(KC):
                        last = e.matmul(bank[:, :], xlnTB[:, kc * 128:(kc + 1) * 128],
                                 Wg_sb[:, kc * 2048 + jb * 512: kc * 2048 + (jb + 1) * 512],
                                 start=(kc == 0), stop=(kc == KC - 1))
                    return last
                gpe = T.emit("pe", fng, waits=[xT_tok, _wg_tok] + bgw(bk),
                             incs=(("pe", 1),))[0]
                gev = T.emit("ac", lambda e, jb=jb, bank=bank, par=par: e.activation(
                    g_bf[par][:, jb * 512:(jb + 1) * 512], bank[:, :], AF.Silu,
                    bias=0.0, scale=1.0),
                    waits=[gpe], incs=(("ac", 1),))[0]
                g_toks.append(gev)
                bg[bk] = gev
            b1_toks[t] = (qh_tok, oi_tok, g_toks)

        def phb2(t):
            par = t % 4
            qh_t, oi_tok, g_toks = b1_toks.pop(t)
            oadd_last = None
            for h in range(H):
                bank = ps[4 + (h % 2)]
                bk = f"ps{4 + (h % 2)}"
                def fnc(e, h=h, t=t, bank=bank, cpar=(t // 2) % 3):
                    last = None
                    for dc in range(2):
                        last = e.matmul(bank[:, :],
                                 qhat_in[cpar][:, (2 * h + dc) * CH + (t % 2) * 128:
                                         (2 * h + dc) * CH + (t % 2) * 128 + 128],
                                 S_seed[:, (h * 2 + dc) * DV:(h * 2 + dc + 1) * DV],
                                 start=(dc == 0), stop=(dc == 1))
                    return last
                cpe = T.emit("pe", fnc,
                             waits=[seed_ready, qh_t] + bgw(bk),
                             incs=(("pe", 1),))[0]
                oadd_last = T.emit("dv", lambda e, h=h, bank=bank, par=par: e.scalar_tensor_tensor(
                    o_f32[:, h * DV:(h + 1) * DV], bank[:, :], 1.0,
                    o_in[par][:, h * DV:(h + 1) * DV], ALU.mult, ALU.add),
                    waits=[cpe, oi_tok], incs=(("dv", 1),))[0]
                bg[bk] = oadd_last
                T.emit("ac", lambda e, h=h: e.activation(
                    junk[:, 0:DV], o_f32[:, h * DV:(h + 1) * DV], AF.Square,
                    bias=0.0, scale=1.0, accum_out=ssq[:, h:h + 1]),
                    waits=[oadd_last], incs=(("ac", 1),))
            sq_done = ("ac", T.cur("ac"))
            T.emit("dv", lambda e: e.tensor_scalar(ssq[:, 4:8], ssq[:, 0:4], 1.0 / DV, EPS,
                                                   ALU.mult, ALU.add),
                   waits=[sq_done], incs=(("dv", 1),))
            T.emit("dv", lambda e: e.reciprocal(ssq[:, 0:4], ssq[:, 4:8]), incs=(("dv", 1),))
            rcp = ("dv", T.cur("dv"))
            T.emit("ac", lambda e: e.activation(rh_sb[:, 0:4], ssq[:, 0:4], AF.Sqrt,
                                                bias=0.0, scale=1.0),
                   waits=[rcp], incs=(("ac", 1),))
            r0t = ("ac", T.cur("ac"))
            T.emit("dv", lambda e: e.tensor_tensor(ssq[:, 0:4], rh_sb[:, 0:4], rh_sb[:, 0:4],
                                                   ALU.mult),
                   waits=[r0t], incs=(("dv", 1),))
            T.emit("dv", lambda e: e.tensor_tensor(ssq[:, 0:4], ssq[:, 0:4], ssq[:, 4:8],
                                                   ALU.mult), incs=(("dv", 1),))
            T.emit("dv", lambda e: e.tensor_scalar(ssq[:, 0:4], ssq[:, 0:4], -0.5, 1.5,
                                                   ALU.mult, ALU.add), incs=(("dv", 1),))
            T.emit("dv", lambda e: e.tensor_tensor(rh_sb[:, 0:4], rh_sb[:, 0:4], ssq[:, 0:4],
                                                   ALU.mult), incs=(("dv", 1),))
            rh_done = ("dv", T.cur("dv"))
            T.pend["ac"].append(rh_done)
            og_last = None
            for h in range(H):
                og_last = T.emit("dv", lambda e, h=h, par=par: e.scalar_tensor_tensor(
                    g_bf[par][:, h * DV:(h + 1) * DV], o_f32[:, h * DV:(h + 1) * DV],
                    rh_sb[:, 0 + h:1 + h], g_bf[par][:, h * DV:(h + 1) * DV],
                    ALU.mult, ALU.mult),
                    waits=[rh_done, g_toks[h]], incs=(("dv", 1),))[0]
            og_last_gp = og_last
            t_last = None
            for half in range(2):
                ppe = T.emit("pe", lambda e, half=half, par=par: [
                    e.transpose(pt1[:, i * 128:(i + 1) * 128],
                                g_bf[par][:, half * 1024 + i * 128: half * 1024 + (i + 1) * 128],
                                ident_bf[:, :]) for i in range(8)][-1],
                    waits=[og_last, og_last_gp] + bgw("pt1"), incs=(("pe", 1),))[0]
                t_last = T.emit("dv", lambda e, half=half: e.tensor_copy(
                    ogT[:, half * 1024:(half + 1) * 1024], pt1[:, :]),
                    waits=[ppe], incs=(("dv", 1),))[0]
                bg["pt1"] = t_last
            for jb in range(2):
                bank = ps[jb]
                def fny(e, jb=jb, bank=bank):
                    last = None
                    for vc in range(16):
                        last = e.matmul(bank[:, :], ogT[:, vc * 128:(vc + 1) * 128],
                                 Wo_sb[:, vc * D + jb * 512: vc * D + (jb + 1) * 512],
                                 start=(vc == 0), stop=(vc == 15))
                    return last
                ype = T.emit("pe", fny, waits=[t_last, wo_ready] + bgw(f"ps{jb}"),
                             incs=(("pe", 1),))[0]
                yev = T.emit("dv", lambda e, t=t, jb=jb, bank=bank: e.scalar_tensor_tensor(
                    x_sb[:, t * D + jb * 512: t * D + (jb + 1) * 512], bank[:, :], 1.0,
                    x_sb[:, t * D + jb * 512: t * D + (jb + 1) * 512], ALU.mult, ALU.add),
                    waits=[ype], incs=(("dv", 1),))[0]
                bg[f"ps{jb}"] = yev
            ln_stats(t, scratch=ogT)
            T.pend["sy"].append(("dv", T.cur("dv")))
            T.pend["sy"].append(("pe", T.cur("pe")))

        phb1(0)
        phb1(1)
        phb1(2)
        for t in range(NT):
            if t + 3 < NT:
                phb1(t + 3)
            phb2(t)
        T.barrier()

    ln3_r = rsqrt_chain(2, D)

    with ExitStack() as ffn2_stk:
        Mh2 = ffn2_stk.enter_context(nc.sbuf_tensor("Mh2", [128, KC * D], BF16))
        Ml2 = ffn2_stk.enter_context(nc.sbuf_tensor("Ml2", [128, KC * D], BF16))
        xln_tm2 = [ffn2_stk.enter_context(nc.sbuf_tensor(f"xln_tm2{p}", [128, D], BF16))
                   for p in range(2)]
        xlnT2b = [ffn2_stk.enter_context(nc.sbuf_tensor(f"xlnT2b{p}", [128, D], BF16))
                  for p in range(2)]
        brow2 = ffn2_stk.enter_context(nc.sbuf_tensor("brow2", [128, D], BF16))
        brow2_tok = dma("gld", "gp", brow2[0:1, :], dbias[:, :])
        m2_ld = []
        for i, dst in enumerate((Mh2, Ml2)):
            for r in range(NCORES):
                m2_ld.append(dma("ld", "sy",
                    bass.AP(dst, 128 * r, [[KC * D, 128], [D, KC], [1, 128]]),
                    bass.AP(cc2_out, (r * 2 + i) * D * 128,
                            [[128, 128], [128 * 128, KC], [1, 128]])))
        ffn_pass(2, Mh2, Ml2, m2_ld, xln_tm2, xlnT2b, ln3_r, do_stats=False, do_out=True,
                 brow=brow2, brow_tok=brow2_tok)

    T.emit("sy", lambda e: e.nop(), waits=list(out_toks), incs=(("gp", 1),))

    with nc.Block() as block:
        T.realize(block, semh)
    stk.close()
    return nc


_NC_CACHE = None


def _get_nc():
    global _NC_CACHE
    if _NC_CACHE is None:
        _NC_CACHE = build_nc()
    return _NC_CACHE


def kernel(L0_attn_x, L0_attn_Wq, L0_attn_Wk, L0_attn_Wv, L0_attn_Wg, L0_attn_Wo,
           L0_attn_gn_weight, L0_ffn_w_freq_real, L0_ffn_w_freq_imag, L0_ffn_bias):
    c = host_constants()
    x = np.ascontiguousarray(np.asarray(L0_attn_x, dtype=np.float32))
    f32 = lambda a: np.ascontiguousarray(np.asarray(a, dtype=np.float32))
    gnw = np.asarray(L0_attn_gn_weight, dtype=np.float32)
    vc_idx = (np.arange(16)[None, :] * 128 + np.arange(128)[:, None]) % 512
    gnwt = np.ascontiguousarray(gnw[vc_idx].astype(np.float32))
    shared = {
        "wq": f32(L0_attn_Wq), "wk": f32(L0_attn_Wk), "wv": f32(L0_attn_Wv),
        "wg": f32(L0_attn_Wg), "wo": f32(L0_attn_Wo),
        "wfr": f32(L0_ffn_w_freq_real), "wfi": f32(L0_ffn_w_freq_imag),
        "bias": f32(L0_ffn_bias).reshape(1, D),
        "gnwt": gnwt,
        "dts": c["dts"].astype(np.float32),
        "cqb": c["cqb"].astype(np.float32),
        "ckb": c["ckb"].astype(np.float32),
        "identc": c["ident"].astype(np.float32),
    }
    in_maps = []
    for core in range(NCORES):
        b, half = core // 2, core % 2
        pos = slice(half * TOK, (half + 1) * TOK)
        m = dict(shared)
        m["x"] = np.ascontiguousarray(x[b, pos, :])
        m["frs"] = np.ascontiguousarray(c["C"][:, core * 128:(core + 1) * 128].astype(np.float32))
        m["fis"] = np.ascontiguousarray(c["S"][:, core * 128:(core + 1) * 128].astype(np.float32))
        m["fisn"] = np.ascontiguousarray((-c["S"][:, core * 128:(core + 1) * 128]).astype(np.float32))
        m["cosf"] = np.ascontiguousarray(c["cos"][pos, :].T.astype(np.float32))
        m["sinf"] = np.ascontiguousarray(c["sin"][pos, :].T.astype(np.float32))
        m["flagcol"] = np.full((128, 1), float(half), dtype=np.float32)
        in_maps.append(m)
    nc = _get_nc()
    res = run_bass_kernel_spmd(nc, in_maps, core_ids=list(range(NCORES)))
    out = np.empty((4, 4096, D), dtype=np.float32)
    for core in range(NCORES):
        b, half = core // 2, core % 2
        out[b, half * TOK:(half + 1) * TOK, :] = res.results[core]["y"]
    return out


# version 13
# speedup vs baseline: 1.0011x; 1.0011x over previous
import numpy as np
from contextlib import ExitStack
import concourse.bass as bass
import concourse.mybir as mybir
from concourse.bass_utils import run_bass_kernel_spmd

F32 = mybir.dt.float32
BF16 = mybir.dt.bfloat16
AF = mybir.ActivationFunctionType
ALU = mybir.AluOpType
AXX = mybir.AxisListType.X

NCORES = 8
D = 1024
TOK = 2048
NT = 16
H = 4
DK = 256
DV = 512
CH = 256
NCH = TOK // CH
EPS = 1e-5
KC = 8

ENGS = ("pe", "dv", "ac", "gp", "sy")


class Tape:

    def __init__(self):
        self.rows = []
        self.cnt = {}
        self.pend = {e: [] for e in ENGS}

    def cur(self, sem):
        return self.cnt.get(sem, 0)

    def emit(self, eng, fn, waits=(), incs=()):
        w = self.pend[eng] + list(waits)
        self.pend[eng] = []
        if eng in ("dv", "ac", "gp") and self.cur(eng) > 0:
            w.append((eng, self.cur(eng)))
        for s, v in w:
            assert v <= self.cur(s), f"wait {s}>={v} not satisfiable (cur {self.cur(s)})"
        self.rows.append((eng, fn, w, list(incs)))
        for s, n in incs:
            self.cnt[s] = self.cur(s) + n
        return [(s, self.cur(s)) for s, _ in incs]

    def barrier(self):
        dma_waits = {"sy": [], "gp": []}
        for s, v in self.cnt.items():
            if s.startswith("syd") and v > 0:
                dma_waits["sy"].append((s, v))
            elif s.startswith("gpd") and v > 0:
                dma_waits["gp"].append((s, v))
        for e in ENGS:
            self.emit(e, lambda en: en.nop(), waits=dma_waits.get(e, ()),
                      incs=(("bar", 1),))
        v = self.cur("bar")
        for e in ENGS:
            self.pend[e].append(("bar", v))

    def realize(self, block, semh):
        for eng_name, deco in (("pe", block.tensor), ("dv", block.vector),
                               ("ac", block.scalar), ("gp", block.gpsimd),
                               ("sy", block.sync)):
            def mk(eng_name=eng_name):
                def emitter(e):
                    seen = {}
                    for g, fn, waits, incs in self.rows:
                        if g != eng_name:
                            continue
                        for s, v in waits:
                            if seen.get(s, 0) < v:
                                e.wait_ge(semh[s], v)
                                seen[s] = v
                        inst = fn(e)
                        for s, n in incs:
                            inst.then_inc(semh[s], n)
                return emitter
            deco(mk())


def host_constants():
    c = {}
    n = np.arange(D)
    ang = 2.0 * np.pi * np.outer(n, n) / D
    c["C"] = np.cos(ang)
    c["S"] = -np.sin(ang)
    inv_freq = 1.0 / (10000.0 ** (np.arange(0, DK, 2) / DK))
    t = np.arange(4096)
    fr = np.outer(t, inv_freq)
    c["cos"] = np.cos(fr)
    c["sin"] = np.sin(fr)
    gamma = 1.0 - np.exp2(-5.0 - np.arange(H))
    c["gamma"] = gamma
    idx = np.arange(CH)
    dts = np.zeros((128, H * 2 * CH))
    for h in range(H):
        g = gamma[h]
        DT = np.where(idx[:, None] <= idx[None, :],
                      g ** (idx[None, :] - idx[:, None]), 0.0) / 16.0
        for jt in range(2):
            dts[:, (h * 2 + jt) * CH:(h * 2 + jt + 1) * CH] = DT[jt * 128:(jt + 1) * 128, :]
    c["dts"] = dts
    cqb = np.zeros((128, H * CH))
    ckb = np.zeros((128, H * CH))
    for h in range(H):
        g = gamma[h]
        cqb[:, h * CH:(h + 1) * CH] = (g ** (idx + 1.0) / 16.0)[None, :]
        ckb[:, h * CH:(h + 1) * CH] = (g ** (CH - 1.0 - idx))[None, :]
    c["cqb"] = cqb
    c["ckb"] = ckb
    c["ident"] = np.eye(128)
    return c


def build_nc(debug=False):
    nc = bass.Bass("TRN2", target_bir_lowering=False)
    dbg_kind = "ExternalOutput" if debug else "Internal"

    dx = nc.dram_tensor("x", [TOK, D], F32, kind="ExternalInput")
    dwq = nc.dram_tensor("wq", [D, D], F32, kind="ExternalInput")
    dwk = nc.dram_tensor("wk", [D, D], F32, kind="ExternalInput")
    dwv = nc.dram_tensor("wv", [D, 2048], F32, kind="ExternalInput")
    dwg = nc.dram_tensor("wg", [D, 2048], F32, kind="ExternalInput")
    dwo = nc.dram_tensor("wo", [2048, D], F32, kind="ExternalInput")
    dwfr = nc.dram_tensor("wfr", [D, D], F32, kind="ExternalInput")
    dwfi = nc.dram_tensor("wfi", [D, D], F32, kind="ExternalInput")
    dbias = nc.dram_tensor("bias", [1, D], F32, kind="ExternalInput")
    dgnwt = nc.dram_tensor("gnwt", [128, 16], F32, kind="ExternalInput")
    dfrs = nc.dram_tensor("frs", [D, 128], F32, kind="ExternalInput")
    dfis = nc.dram_tensor("fis", [D, 128], F32, kind="ExternalInput")
    dfisn = nc.dram_tensor("fisn", [D, 128], F32, kind="ExternalInput")
    dcos = nc.dram_tensor("cosf", [128, TOK], F32, kind="ExternalInput")
    dsin = nc.dram_tensor("sinf", [128, TOK], F32, kind="ExternalInput")
    ddts = nc.dram_tensor("dts", [128, H * 2 * CH], F32, kind="ExternalInput")
    dcqb = nc.dram_tensor("cqb", [128, H * CH], F32, kind="ExternalInput")
    dckb = nc.dram_tensor("ckb", [128, H * CH], F32, kind="ExternalInput")
    dident = nc.dram_tensor("identc", [128, 128], F32, kind="ExternalInput")
    dflag = nc.dram_tensor("flagcol", [128, 1], F32, kind="ExternalInput")
    dy = nc.dram_tensor("y", [TOK, D], F32, kind="ExternalOutput")

    o_hbm = nc.dram_tensor("o_hbm", [TOK, 2048], BF16, kind=dbg_kind)
    qhat_hbm = nc.dram_tensor("qhat_hbm", [D, TOK], BF16, kind=dbg_kind)
    cc1_in = nc.dram_tensor("cc1_in", [4, D, 128], BF16)
    cc1_out = nc.dram_tensor("cc1_out", [NCORES * 4, D, 128], BF16, addr_space="Shared")
    cc2_in = nc.dram_tensor("cc2_in", [2, D, 128], BF16)
    cc2_out = nc.dram_tensor("cc2_out", [NCORES * 2, D, 128], BF16, addr_space="Shared")
    cc3_in = nc.dram_tensor("cc3_in", [H * 2 * 128, DV], F32, kind=dbg_kind)
    dbg_q = nc.dram_tensor("dbg_q", [128, 4096], BF16, kind=dbg_kind)
    cc3_out = nc.dram_tensor("cc3_out", [2 * H * 2 * 128, DV], F32)

    gamma = 1.0 - np.exp2(-5.0 - np.arange(H))
    gC = [float(gamma[h] ** CH) for h in range(H)]
    ghat = [[float(gamma[h] ** (CH * cc)) for cc in range(NCH)] for h in range(H)]

    T = Tape()
    stk = ExitStack()
    bg = {}

    def bgw(*keys):
        return [bg[k] for k in keys if k in bg]

    def sb(name, cols, dtype):
        return stk.enter_context(nc.sbuf_tensor(name, [128, cols], dtype))

    semnames = ([f"syd{i}" for i in range(12)] + [f"gpd{i}" for i in range(6)]
                + ["pe", "dv", "ac", "gp", "cc", "bar"])
    semh = {s: stk.enter_context(nc.semaphore(s)) for s in semnames}

    x_sb = sb("x_sb", NT * D, F32)
    cqb_sb = sb("cqb_sb", H * CH, BF16)
    ckb_sb = sb("ckb_sb", H * CH, BF16)
    ident_bf = sb("ident_bf", 128, BF16)
    gnwt_sb = sb("gnwt_sb", 16, F32)
    flag_sb = sb("flag_sb", 1, F32)
    ones_row = sb("ones_row", 128, BF16)
    junk = sb("junk", D, BF16)
    stats = sb("stats", 64, F32)
    r_ln = sb("r_ln", 3 * NT, F32)
    nmr_ln = sb("nmr_ln", 3 * NT, F32)
    ssq = sb("ssq", 8, F32)
    rh_sb = sb("rh_sb", 4, F32)

    ps = [stk.enter_context(nc.psum_tensor(f"ps{i}", [128, 512], F32)) for i in range(6)]
    pt0 = stk.enter_context(nc.psum_tensor("pt0", [128, 1024], BF16))
    pt1 = stk.enter_context(nc.psum_tensor("pt1", [128, 1024], BF16))

    _dma_pool = {"sy": [f"syd{i}" for i in range(12)],
                 "gp": [f"gpd{i}" for i in range(6)]}
    _dma_rr = {"sy": 0, "gp": 0}

    def dma(_legacy, eng, out, in_, waits=()):
        pool = _dma_pool[eng]
        s = pool[_dma_rr[eng] % len(pool)]
        _dma_rr[eng] += 1
        prev = T.cur(s)
        return T.emit(eng, lambda e, o=out, i=in_: e.dma_start(out=o, in_=i),
                      waits=list(waits) + [(s, prev)], incs=((s, 16),))[0]

    w_x = dma("ld", "sy", x_sb[:, :],
              bass.AP(dx, 0, [[D, 128], [128 * D, NT], [1, D]]))
    w_cqb = dma("gld", "gp", cqb_sb[:, :], dcqb[:, :])
    w_ckb = dma("gld", "gp", ckb_sb[:, :], dckb[:, :])
    w_id = dma("gld", "gp", ident_bf[:, :], dident[:, :])
    w_gnw = dma("ld", "sy", gnwt_sb[:, :], dgnwt[:, :])
    w_flag = dma("ld", "sy", flag_sb[:, :], dflag[:, :])
    T.emit("dv", lambda e: e.memset(ones_row[0:1, :], 0.5), incs=(("dv", 1),))
    ones_tok = ("dv", T.cur("dv"))

    def ln_stats(tile, waits_dv=(), waits_ac=None, scratch=None, eng="dv"):
        scr = junk if scratch is None else scratch
        T.emit(eng, lambda e, t=tile: e.reduce_sum(
            stats[:, t:t + 1], x_sb[:, t * D:(t + 1) * D], axis=AXX),
            waits=waits_dv, incs=((eng, 1),))
        T.emit(eng, lambda e, t=tile, scr=scr: e.tensor_tensor(
            scr[:, 0:D], x_sb[:, t * D:(t + 1) * D], x_sb[:, t * D:(t + 1) * D], ALU.mult),
            incs=((eng, 1),))
        T.emit(eng, lambda e, t=tile, scr=scr: e.reduce_sum(
            stats[:, 16 + t:17 + t], scr[:, 0:D], axis=AXX),
            incs=((eng, 1),))

    def rsqrt_chain(slot, nfeat, waits=()):
        a = stats[:, 32:48]
        b = stats[:, 48:64]
        sl = slice(slot * NT, slot * NT + NT)
        inv = 1.0 / nfeat
        T.emit("dv", lambda e: e.tensor_scalar_mul(a, stats[:, 0:16], inv),
               waits=waits, incs=(("dv", 1),))
        T.emit("dv", lambda e: e.tensor_scalar_mul(b, stats[:, 16:32], inv), incs=(("dv", 1),))
        T.emit("dv", lambda e: e.tensor_tensor(stats[:, 0:16], a, a, ALU.mult), incs=(("dv", 1),))
        T.emit("dv", lambda e: e.tensor_tensor(b, b, stats[:, 0:16], ALU.subtract), incs=(("dv", 1),))
        T.emit("dv", lambda e: e.tensor_scalar_add(b, b, EPS), incs=(("dv", 1),))
        T.emit("dv", lambda e: e.reciprocal(stats[:, 0:16], b), incs=(("dv", 1),))
        tok_dv = ("dv", T.cur("dv"))
        T.emit("ac", lambda e: e.activation(r_ln[:, sl], stats[:, 0:16], AF.Sqrt,
                                            bias=0.0, scale=1.0),
               waits=[tok_dv], incs=(("ac", 1),))
        tok_ac = ("ac", T.cur("ac"))
        T.emit("dv", lambda e: e.tensor_tensor(stats[:, 0:16], r_ln[:, sl], r_ln[:, sl], ALU.mult),
               waits=[tok_ac], incs=(("dv", 1),))
        T.emit("dv", lambda e: e.tensor_tensor(stats[:, 0:16], stats[:, 0:16], b, ALU.mult),
               incs=(("dv", 1),))
        T.emit("dv", lambda e: e.tensor_scalar(stats[:, 0:16], stats[:, 0:16], -0.5, 1.5,
                                               ALU.mult, ALU.add), incs=(("dv", 1),))
        T.emit("dv", lambda e: e.tensor_tensor(r_ln[:, sl], r_ln[:, sl], stats[:, 0:16], ALU.mult),
               incs=(("dv", 1),))
        T.emit("dv", lambda e: e.tensor_tensor(stats[:, 0:16], a, r_ln[:, sl], ALU.mult),
               incs=(("dv", 1),))
        T.emit("dv", lambda e: e.tensor_scalar_mul(nmr_ln[:, sl], stats[:, 0:16], -1.0),
               incs=(("dv", 1),))
        return ("dv", T.cur("dv"))

    for t in range(NT):
        ln_stats(t, waits_dv=[w_x], waits_ac=[w_x])
    ln1_stats = [("dv", T.cur("dv")), ("ac", T.cur("ac"))]

    with ExitStack() as form_stk:
        def fsb(name, cols, dtype=BF16):
            return form_stk.enter_context(nc.sbuf_tensor(name, [128, cols], dtype))
        frsh = fsb("frsh", 1024); frsl = fsb("frsl", 1024)
        fish = fsb("fish", 1024); fisl = fsb("fisl", 1024)
        fisnh = fsb("fisnh", 1024); fisnl = fsb("fisnl", 1024)
        ath = fsb("ath", 1024); atl = fsb("atl", 1024)
        bth = fsb("bth", 1024); btl = fsb("btl", 1024)
        mhb = fsb("mhb", 1024); mlb = fsb("mlb", 1024)
        stg2 = fsb("stg2", 1024, F32)

        def slice_ap(dt_):
            return bass.AP(dt_, 0, [[128, 128], [128 * 128, KC], [1, 128]])
        for drm, hh, ll in ((dfrs, frsh, frsl), (dfis, fish, fisl), (dfisn, fisnh, fisnl)):
            wtok = dma("gld", "gp", hh[:, :], slice_ap(drm))
            wtok2 = dma("ld", "sy", stg2[:, :], slice_ap(drm))
            dv_tok = T.emit("dv", lambda e, hh=hh, ll=ll: e.tensor_tensor(
                ll[:, :], stg2[:, :], hh[:, :], ALU.subtract),
                waits=[wtok, wtok2], incs=(("dv", 1),))[0]
            T.pend["sy"].append(dv_tok)
        fsplit_done = [("dv", T.cur("dv"))]

        with ExitStack() as wsplit_stk:
            def wsb(name):
                return wsplit_stk.enter_context(nc.sbuf_tensor(name, [128, KC * D], BF16))
            wrh, wrl = wsb("wrh"), wsb("wrl")
            wih, wil = wsb("wih"), wsb("wil")
            stgc = wsplit_stk.enter_context(nc.sbuf_tensor("stgc", [128, D], F32))
            for dw, whh, wll in ((dwfr, wrh, wrl), (dwfi, wih, wil)):
                wh_tok = dma("gld", "gp", whh[:, :],
                             bass.AP(dw, 0, [[D, 128], [128 * D, KC], [1, D]]))
                for kc in range(KC):
                    s_tok = dma("ld", "sy", stgc[:, :],
                                bass.AP(dw, kc * 128 * D, [[D, 128], [1, D]]))
                    dv_tok = T.emit("dv", lambda e, kc=kc, wll=wll, whh=whh: e.tensor_tensor(
                        wll[:, kc * D:(kc + 1) * D], stgc[:, :],
                        whh[:, kc * D:(kc + 1) * D], ALU.subtract),
                        waits=[s_tok, wh_tok], incs=(("dv", 1),))[0]
                    T.pend["sy"].append(dv_tok)
            splits_done = fsplit_done + [("dv", T.cur("dv"))]

            def mform_stage(dst_h, dst_l, opnds, scale=None):
                for mt in range(KC):
                    bank = ps[mt % 4]
                    bkey = f"ps{mt % 4}"
                    def fn(e, mt=mt, bank=bank, opnds=tuple(opnds)):
                        n = len(opnds) * KC
                        i = 0
                        last = None
                        for wH, cH in opnds:
                            for kc in range(KC):
                                last = e.matmul(bank[:, 0:128],
                                         wH[:, kc * D + mt * 128: kc * D + mt * 128 + 128],
                                         cH[:, kc * 128:(kc + 1) * 128],
                                         start=(i == 0), stop=(i == n - 1))
                                i += 1
                        return last
                    pe_tok = T.emit("pe", fn, waits=list(splits_done) + bgw(bkey),
                                    incs=(("pe", 1),))[0]
                    if scale is None:
                        T.emit("dv", lambda e, bank=bank, mt=mt, dst_h=dst_h: e.tensor_copy(
                            dst_h[:, mt * 128:(mt + 1) * 128], bank[:, 0:128]),
                            waits=[pe_tok], incs=(("dv", 1),))
                        d2 = T.emit("dv", lambda e, bank=bank, mt=mt, dst_h=dst_h, dst_l=dst_l:
                                    e.scalar_tensor_tensor(
                                        dst_l[:, mt * 128:(mt + 1) * 128], bank[:, 0:128], 1.0,
                                        dst_h[:, mt * 128:(mt + 1) * 128], ALU.mult, ALU.subtract),
                                    incs=(("dv", 1),))[0]
                    else:
                        T.emit("dv", lambda e, bank=bank, mt=mt, dst_h=dst_h: e.tensor_scalar_mul(
                            dst_h[:, mt * 128:(mt + 1) * 128], bank[:, 0:128], scale),
                            waits=[pe_tok], incs=(("dv", 1),))
                        d2 = T.emit("dv", lambda e, bank=bank, mt=mt, dst_h=dst_h, dst_l=dst_l:
                                    e.scalar_tensor_tensor(
                                        dst_l[:, mt * 128:(mt + 1) * 128], bank[:, 0:128], scale,
                                        dst_h[:, mt * 128:(mt + 1) * 128], ALU.mult, ALU.subtract),
                                    incs=(("dv", 1),))[0]
                    bg[bkey] = d2

            mform_stage(ath, atl, [(wrh, frsh), (wrl, frsh), (wrh, frsl),
                                   (wih, fisnh), (wil, fisnh), (wih, fisnl)])
            mform_stage(bth, btl, [(wih, frsh), (wil, frsh), (wih, frsl),
                                   (wrh, fish), (wrl, fish), (wrh, fisl)])
            s1_done = [("dv", T.cur("dv"))]
            ag1_st_toks = [dma("st", "sy",
                    bass.AP(cc1_in, i * D * 128, [[128, 128], [128 * 128, KC], [1, 128]]),
                    srct[:, :], waits=s1_done if i == 0 else ())
                for i, srct in enumerate((ath, atl, bth, btl))]
            T.emit("gp", lambda e: e.collective_compute(
                "AllGather", ALU.bypass, replica_groups=[list(range(NCORES))],
                ins=[cc1_in.ap().opt()], outs=[cc1_out.ap().opt()]),
                waits=ag1_st_toks, incs=(("cc", 1),))
            ag1_done = ("cc", T.cur("cc"))
            T.barrier()

        atfh = fsb("atfh", KC * D); atfl = fsb("atfl", KC * D)
        btfh = fsb("btfh", KC * D); btfl = fsb("btfl", KC * D)
        agload = []
        for ti, dst in enumerate((atfh, atfl, btfh, btfl)):
            for r in range(NCORES):
                agload.append(dma("ld", "sy",
                    bass.AP(dst, 128 * r, [[KC * D, 128], [D, KC], [1, 128]]),
                    bass.AP(cc1_out, (r * 4 + ti) * D * 128,
                            [[128, 128], [128 * 128, KC], [1, 128]]),
                    waits=[ag1_done]))

        mscale = 0.5 / D
        saved = T.pend["pe"]
        T.pend["pe"] = saved + list(agload)
        mform_stage(mhb, mlb, [(atfh, frsh), (atfl, frsh), (atfh, frsl),
                               (btfh, fish), (btfl, fish), (btfh, fisl)], scale=mscale)
        s2_done = [("dv", T.cur("dv"))]
        ag2_st_toks = [dma("st", "sy",
                bass.AP(cc2_in, i * D * 128, [[128, 128], [128 * 128, KC], [1, 128]]),
                srct[:, :], waits=s2_done if i == 0 else ())
            for i, srct in enumerate((mhb, mlb))]
        T.emit("gp", lambda e: e.collective_compute(
            "AllGather", ALU.bypass, replica_groups=[list(range(NCORES))],
            ins=[cc2_in.ap().opt()], outs=[cc2_out.ap().opt()]),
            waits=ag2_st_toks, incs=(("cc", 1),))
        ag2_done = ("cc", T.cur("cc"))
        T.barrier()

    ln1_r = rsqrt_chain(0, D, waits=ln1_stats)

    out_toks = []

    def ffn_pass(slot, msbh, msbl, m_ld, xln_tm, xlnT, ln_tok, do_stats, do_out,
                 brow, brow_tok):
        sl0 = slot * NT
        prep_toks = {}
        prep_pe = {}

        def prep(t):
            par = t % 2
            guard = [prep_pe[t - 2]] if (t - 2) in prep_pe else []
            a_tok = T.emit("ac", lambda e, t=t, par=par: e.activation(
                xln_tm[par][:, :], x_sb[:, t * D:(t + 1) * D], AF.Identity,
                bias=nmr_ln[:, sl0 + t:sl0 + t + 1], scale=r_ln[:, sl0 + t:sl0 + t + 1]),
                waits=[ln_tok] + guard, incs=(("ac", 1),))[0]
            pe_tok = T.emit("pe", lambda e, par=par: [
                e.transpose(pt0[:, kc * 128:(kc + 1) * 128],
                            xln_tm[par][:, kc * 128:(kc + 1) * 128], ident_bf[:, :])
                for kc in range(KC)][-1],
                waits=[a_tok, w_id] + bgw("pt0"), incs=(("pe", 1),))[0]
            dv_tok = T.emit("dv", lambda e, par=par: e.tensor_copy(xlnT[par][:, :], pt0[:, :]),
                            waits=[pe_tok], incs=(("dv", 1),))[0]
            bg["pt0"] = dv_tok
            prep_toks[t] = dv_tok
            prep_pe[t] = pe_tok

        prep(0)
        for t in range(NT):
            if t + 1 < NT:
                prep(t + 1)
            par = t % 2
            dv_tok = prep_toks.pop(t)
            ev_toks = []
            banks = (ps[(t % 2) * 2], ps[(t % 2) * 2 + 1])
            bkeys = (f"ps{(t % 2) * 2}", f"ps{(t % 2) * 2 + 1}")
            def fn(e, banks=banks, par=par):
                for jb in range(2):
                    e.matmul(banks[jb][:, :], ones_row[0:1, :],
                             brow[0:1, jb * 512:(jb + 1) * 512], start=True, stop=False)
                last = None
                for i, msb in enumerate((msbh, msbl)):
                    for kc in range(KC):
                        for jb in range(2):
                            last = e.matmul(banks[jb][:, :],
                                     xlnT[par][:, kc * 128:(kc + 1) * 128],
                                     msb[:, kc * D + jb * 512: kc * D + (jb + 1) * 512],
                                     start=False, stop=(i == 1 and kc == KC - 1))
                return last
            pe2 = T.emit("pe", fn,
                         waits=[dv_tok, brow_tok, ones_tok] + list(m_ld) + bgw(*bkeys),
                         incs=(("pe", 1),))[0]
            for jb in range(2):
                ev = T.emit("dv", lambda e, t=t, jb=jb, bank=banks[jb]: e.scalar_tensor_tensor(
                    x_sb[:, t * D + jb * 512: t * D + (jb + 1) * 512], bank[:, :], 1.0,
                    x_sb[:, t * D + jb * 512: t * D + (jb + 1) * 512], ALU.mult, ALU.add),
                    waits=[pe2], incs=(("dv", 1),))[0]
                ev_toks.append(ev)
                bg[bkeys[jb]] = ev
            if do_stats:
                ln_stats(t)
            if do_out:
                out_toks.append(dma("st", "sy",
                    bass.AP(dy, t * 128 * D, [[D, 128], [1, D]]),
                    x_sb[:, t * D:(t + 1) * D], waits=ev_toks))

    with ExitStack() as wqkv_stk:
        def qsb(name, cols):
            return wqkv_stk.enter_context(nc.sbuf_tensor(name, [128, cols], BF16))
        Wq_sb = qsb("Wq_sb", KC * D)
        Wk_sb = qsb("Wk_sb", KC * D)
        Wv_sb = qsb("Wv_sb", KC * 2048)
        _wts = [dma("gld", "gp", dst[:, :],
                    bass.AP(srct, 0, [[width, 128], [128 * width, KC], [1, width]]))
                for dst, srct, width in ((Wq_sb, dwq, D), (Wk_sb, dwk, D), (Wv_sb, dwv, 2048))]
        wqkv_loaded = _wts

        with ExitStack() as ffn1_stk:
            Mh_sb = ffn1_stk.enter_context(nc.sbuf_tensor("Mh_sb", [128, KC * D], BF16))
            Ml_sb = ffn1_stk.enter_context(nc.sbuf_tensor("Ml_sb", [128, KC * D], BF16))
            xln_tm1 = [ffn1_stk.enter_context(nc.sbuf_tensor(f"xln_tm1{p}", [128, D], BF16))
                       for p in range(2)]
            xlnT1 = [ffn1_stk.enter_context(nc.sbuf_tensor(f"xlnT1{p}", [128, D], BF16))
                     for p in range(2)]
            brow1 = ffn1_stk.enter_context(nc.sbuf_tensor("brow1", [128, D], BF16))
            brow1_tok = dma("gld", "gp", brow1[0:1, :], dbias[:, :])
            m_ld = []
            for i, dst in enumerate((Mh_sb, Ml_sb)):
                for r in range(NCORES):
                    m_ld.append(dma("ld", "sy",
                        bass.AP(dst, 128 * r, [[KC * D, 128], [D, KC], [1, 128]]),
                        bass.AP(cc2_out, (r * 2 + i) * D * 128,
                                [[128, 128], [128 * 128, KC], [1, 128]]),
                        waits=[ag2_done]))
            ffn_pass(0, Mh_sb, Ml_sb, m_ld, xln_tm1, xlnT1, ln1_r, do_stats=True,
                     do_out=False, brow=brow1, brow_tok=brow1_tok)
            T.barrier()
        ln2_r = rsqrt_chain(1, D)

        with ExitStack() as pa_stk:
            def asb(name, cols, dtype=BF16):
                return pa_stk.enter_context(nc.sbuf_tensor(name, [128, cols], dtype))
            S32 = asb("S32", H * 2 * DV, F32)
            dts_sb = asb("dts_sb", H * 2 * CH, F32)
            w_dts = dma("ld", "sy", dts_sb[:, :], ddts[:, :])
            S_bf = asb("S_bf", H * 2 * DV)
            xln2_tm = [asb(f"xln2_tm{p}", D) for p in range(2)]
            xlnT2 = asb("xlnT2", KC * CH)
            qh_raw = [asb(f"qh_raw{p}", 512) for p in range(2)]
            kh_raw = [asb(f"kh_raw{p}", 512) for p in range(2)]
            qh_fm = [asb(f"qh_fm{p}", 512) for p in range(2)]
            qt_fm = [asb(f"qt_fm{p}", 512) for p in range(2)]
            kh_fm = [asb(f"kh_fm{p}", 512) for p in range(2)]
            khat_tm = [asb(f"khat_tm{p}", 512) for p in range(2)]
            vh_tm = [asb(f"vh_tm{p}", 2 * DV) for p in range(2)]
            attnT_sb = [asb(f"attnT_sb{p}", 2 * 256) for p in range(2)]
            o_stage = asb("o_stage", 2 * 2048)
            qhat_stage = asb("qhat_stage", 2048)
            cos_c = [asb(f"cos_c{p}", CH) for p in range(2)]
            sin_c = [asb(f"sin_c{p}", CH) for p in range(2)]

            T.emit("dv", lambda e: e.memset(S32[:, :], 0.0), incs=(("dv", 1),))
            T.emit("dv", lambda e: e.memset(S_bf[:, :], 0.0), incs=(("dv", 1),))
            sinit = [("dv", T.cur("dv"))]

            prev_rot_q = [None, None]
            prev_rot_k = [None, None]
            part1_toks = {}

            def part1(c, h, xlnT2_ready):
                par = h % 2
                def qk_proj(W, bank, h=h):
                    def fn(e, W=W, bank=bank, h=h):
                        last = None
                        for dc in range(2):
                            for kc in range(KC):
                                last = e.matmul(bank[:, dc * 256:(dc + 1) * 256],
                                         W[:, kc * D + h * 256 + dc * 128:
                                           kc * D + h * 256 + dc * 128 + 128],
                                         xlnT2[:, kc * CH:(kc + 1) * CH],
                                         start=(kc == 0), stop=(kc == KC - 1))
                        return last
                    return fn
                qpe = T.emit("pe", qk_proj(Wq_sb, ps[0]),
                             waits=[xlnT2_ready] + wqkv_loaded + bgw("ps0"),
                             incs=(("pe", 1),))[0]
                qev = T.emit("ac", lambda e, par=par: e.copy(qh_raw[par][:, :], ps[0][:, :]),
                             waits=[qpe] + ([prev_rot_q[par]] if prev_rot_q[par] else []),
                             incs=(("ac", 1),))[0]
                bg["ps0"] = qev
                kpe = T.emit("pe", qk_proj(Wk_sb, ps[1]), waits=bgw("ps1"),
                             incs=(("pe", 1),))[0]
                kev = T.emit("ac", lambda e, par=par: e.copy(kh_raw[par][:, :], ps[1][:, :]),
                             waits=[kpe] + ([prev_rot_k[par]] if prev_rot_k[par] else []),
                             incs=(("ac", 1),))[0]
                bg["ps1"] = kev
                vev_toks = []
                for jt in range(2):
                    bank = ps[2 + jt]
                    def fnv(e, jt=jt, bank=bank, h=h):
                        last = None
                        for kc in range(KC):
                            last = e.matmul(bank[:, :],
                                     xlnT2[:, kc * CH + jt * 128: kc * CH + jt * 128 + 128],
                                     Wv_sb[:, kc * 2048 + h * DV: kc * 2048 + (h + 1) * DV],
                                     start=(kc == 0), stop=(kc == KC - 1))
                        return last
                    vpe = T.emit("pe", fnv, waits=bgw(f"ps{2 + jt}"),
                                 incs=(("pe", 1),))[0]
                    vev = T.emit("ac", lambda e, jt=jt, bank=bank, par=par: e.copy(
                        vh_tm[par][:, jt * DV:(jt + 1) * DV], bank[:, :]),
                        waits=[vpe], incs=(("ac", 1),))[0]
                    vev_toks.append(vev)
                    bg[f"ps{2 + jt}"] = vev
                part1_toks[(c, h)] = (qev, kev, vev_toks)

            def part2(c, h):
                par = h % 2
                qev, kev, vev_toks = part1_toks.pop((c, h))
                cosl = cos_c[c % 2][:, :]
                sinl = sin_c[c % 2][:, :]
                w_cos, w_sin = rot_toks[c]

                def rot(dst, srcb, wait_tok, eng, so, cosl=cosl, sinl=sinl):
                    T.emit(eng, lambda e, dst=dst, srcb=srcb, cosl=cosl, so=so: e.tensor_tensor(
                        junk[:, so:so + 256], srcb[:, 0:256], cosl, ALU.mult),
                        waits=[wait_tok, w_cos, w_sin], incs=((eng, 1),))
                    T.emit(eng, lambda e, srcb=srcb, sinl=sinl, so=so: e.tensor_tensor(
                        junk[:, so + 256:so + 512], srcb[:, 256:512], sinl, ALU.mult),
                        incs=((eng, 1),))
                    T.emit(eng, lambda e, dst=dst, so=so: e.tensor_tensor(
                        dst[:, 0:256], junk[:, so:so + 256], junk[:, so + 256:so + 512],
                        ALU.subtract),
                        incs=((eng, 1),))
                    T.emit(eng, lambda e, srcb=srcb, cosl=cosl, so=so: e.tensor_tensor(
                        junk[:, so:so + 256], srcb[:, 256:512], cosl, ALU.mult),
                        incs=((eng, 1),))
                    T.emit(eng, lambda e, srcb=srcb, sinl=sinl, so=so: e.tensor_tensor(
                        junk[:, so + 256:so + 512], srcb[:, 0:256], sinl, ALU.mult),
                        incs=((eng, 1),))
                    return T.emit(eng, lambda e, dst=dst, so=so: e.tensor_tensor(
                        dst[:, 256:512], junk[:, so:so + 256], junk[:, so + 256:so + 512],
                        ALU.add),
                        incs=((eng, 1),))[0]

                q_rot = rot(qh_fm[par], qh_raw[par], qev, "dv", 0)
                prev_rot_q[par] = q_rot
                for dc in range(2):
                    T.emit("dv", lambda e, dc=dc, h=h, par=par: e.tensor_tensor(
                        qt_fm[par][:, dc * 256:(dc + 1) * 256],
                        qh_fm[par][:, dc * 256:(dc + 1) * 256],
                        cqb_sb[:, h * CH:(h + 1) * CH], ALU.mult),
                        waits=[w_cqb], incs=(("dv", 1),))
                    T.emit("dv", lambda e, dc=dc, h=h, c=c, par=par: e.tensor_scalar_mul(
                        qhat_stage[:, (2 * h + dc) * 256:(2 * h + dc + 1) * 256],
                        qt_fm[par][:, dc * 256:(dc + 1) * 256], ghat[h][c]),
                        incs=(("dv", 1),))
                qt_done = ("dv", T.cur("dv"))
                k_rot = rot(kh_fm[par], kh_raw[par], kev, "gp", 512)
                for dc in range(2):
                    T.emit("gp", lambda e, dc=dc, h=h, par=par: e.tensor_tensor(
                        kh_raw[par][:, dc * 256:(dc + 1) * 256],
                        kh_fm[par][:, dc * 256:(dc + 1) * 256],
                        ckb_sb[:, h * CH:(h + 1) * CH], ALU.mult),
                        waits=[w_ckb], incs=(("gp", 1),))
                khat_fm_done = ("gp", T.cur("gp"))
                prev_rot_k[par] = khat_fm_done

                kt_pe = T.emit("pe", lambda e, par=par: [
                    e.transpose(pt1[:, (jt * 2 + dc) * 128:(jt * 2 + dc + 1) * 128],
                                kh_raw[par][:, dc * 256 + jt * 128: dc * 256 + jt * 128 + 128],
                                ident_bf[:, :])
                    for jt in range(2) for dc in range(2)][-1],
                    waits=[khat_fm_done, w_id] + bgw("pt1"), incs=(("pe", 1),))[0]
                kt_ev = T.emit("dv", lambda e, par=par: e.tensor_copy(
                    khat_tm[par][:, :], pt1[:, 0:512]),
                    waits=[kt_pe], incs=(("dv", 1),))[0]
                bg["pt1"] = kt_ev

                at_pe = T.emit("pe", lambda e, par=par: [
                    e.matmul(ps[4][:, jt * 256:(jt + 1) * 256],
                             kh_fm[par][:, dc * 256 + jt * 128: dc * 256 + jt * 128 + 128],
                             qh_fm[par][:, dc * 256:(dc + 1) * 256],
                             start=(dc == 0), stop=(dc == 1))
                    for jt in range(2) for dc in range(2)][-1],
                    waits=[q_rot, k_rot] + bgw("ps4"), incs=(("pe", 1),))[0]
                atm_last = None
                for jt in range(2):
                    atm_last = T.emit("dv", lambda e, jt=jt, h=h, par=par: e.tensor_tensor(
                        attnT_sb[par][:, jt * 256:(jt + 1) * 256],
                        ps[4][:, jt * 256:(jt + 1) * 256],
                        dts_sb[:, (h * 2 + jt) * CH:(h * 2 + jt + 1) * CH], ALU.mult),
                        waits=[at_pe, w_dts], incs=(("dv", 1),))[0]
                bg["ps4"] = atm_last

                o_banks = (ps[5], ps[4])
                for it in range(2):
                    bank = o_banks[it]
                    bk = "ps5" if it == 0 else "ps4"
                    def fno(e, it=it, bank=bank, h=h, par=par):
                        for jt in range(2):
                            e.matmul(bank[:, :],
                                     attnT_sb[par][:, jt * 256 + it * 128: jt * 256 + it * 128 + 128],
                                     vh_tm[par][:, jt * DV:(jt + 1) * DV],
                                     start=(jt == 0), stop=False)
                        last = None
                        for dc in range(2):
                            last = e.matmul(bank[:, :],
                                     qt_fm[par][:, dc * 256 + it * 128: dc * 256 + it * 128 + 128],
                                     S_bf[:, (h * 2 + dc) * DV:(h * 2 + dc + 1) * DV],
                                     start=False, stop=(dc == 1))
                        return last
                    ope = T.emit("pe", fno,
                                 waits=[atm_last, qt_done, vev_toks[-1]] + bgw(bk) +
                                       (sinit if (c == 0 and h == 0) else []),
                                 incs=(("pe", 1),))[0]
                    oev = T.emit("dv", lambda e, it=it, bank=bank, h=h, c=c: e.tensor_copy(
                        o_stage[:, it * 2048 + h * DV: it * 2048 + (h + 1) * DV], bank[:, :]),
                        waits=[ope], incs=(("dv", 1),))[0]
                    bg[bk] = oev

                tu_banks = (ps[5], ps[4])
                tu_pe = T.emit("pe", lambda e, tb=tu_banks, par=par: [
                    e.matmul(tb[dc][:, :],
                             khat_tm[par][:, jt * 256 + dc * 128: jt * 256 + dc * 128 + 128],
                             vh_tm[par][:, jt * DV:(jt + 1) * DV],
                             start=(jt == 0), stop=(jt == 1))
                    for dc in range(2) for jt in range(2)][-1],
                    waits=[kt_ev] + bgw("ps5", "ps4"), incs=(("pe", 1),))[0]
                for dc in range(2):
                    tok_s = T.emit("dv", lambda e, dc=dc, h=h, tb=tu_banks: e.scalar_tensor_tensor(
                        S32[:, (h * 2 + dc) * DV:(h * 2 + dc + 1) * DV],
                        S32[:, (h * 2 + dc) * DV:(h * 2 + dc + 1) * DV], gC[h],
                        tb[dc][:, :], ALU.mult, ALU.add),
                        waits=[tu_pe], incs=(("dv", 1),))[0]
                    bg["ps5" if dc == 0 else "ps4"] = tok_s
                T.emit("dv", lambda e, h=h: e.tensor_copy(
                    S_bf[:, h * 2 * DV:(h + 1) * 2 * DV],
                    S32[:, h * 2 * DV:(h + 1) * 2 * DV]),
                    incs=(("dv", 1),))

            rot_toks = {}

            pre_pe = {}

            def chunk_pre(c):
                rot_war = [("dv", T.cur("dv")), ("gp", T.cur("gp"))]
                rot_toks[c] = (
                    dma("gld", "gp", cos_c[c % 2][:, :],
                        bass.AP(dcos, c * CH, [[TOK, 128], [1, CH]]), waits=rot_war),
                    dma("gld", "gp", sin_c[c % 2][:, :],
                        bass.AP(dsin, c * CH, [[TOK, 128], [1, CH]]), waits=rot_war),
                )
                for jt in range(2):
                    t = 2 * c + jt
                    guard = [pre_pe[(c - 1, jt)]] if (c - 1, jt) in pre_pe else []
                    a_tok = T.emit("ac", lambda e, t=t, jt=jt: e.activation(
                        xln2_tm[jt][:, :], x_sb[:, t * D:(t + 1) * D], AF.Identity,
                        bias=nmr_ln[:, NT + t:NT + t + 1], scale=r_ln[:, NT + t:NT + t + 1]),
                        waits=[ln2_r] + guard, incs=(("ac", 1),))[0]
                    pe_tok = T.emit("pe", lambda e, jt=jt: [
                        e.transpose(pt0[:, kc * 128:(kc + 1) * 128],
                                    xln2_tm[jt][:, kc * 128:(kc + 1) * 128],
                                    ident_bf[:, :]) for kc in range(KC)][-1],
                        waits=[a_tok, w_id] + bgw("pt0"), incs=(("pe", 1),))[0]
                    dv_tok = T.emit("dv", lambda e, jt=jt: e.tensor_copy(
                        bass.AP(xlnT2, jt * 128, [[KC * CH, 128], [CH, KC], [1, 128]]),
                        pt0[:, :]),
                        waits=[pe_tok], incs=(("dv", 1),))[0]
                    bg["pt0"] = dv_tok
                    pre_pe[(c, jt)] = pe_tok
                return ("dv", T.cur("dv"))

            pre_tok = chunk_pre(0)
            part1(0, 0, pre_tok)
            for c in range(NCH):
                for h in range(H):
                    if h + 1 < H:
                        part1(c, h + 1, pre_tok)
                    elif c + 1 < NCH:
                        pre_tok = chunk_pre(c + 1)
                        part1(c + 1, 0, pre_tok)
                    part2(c, h)

                ostage_done = ("dv", T.cur("dv"))
                t1 = dma("st", "sy",
                         bass.AP(o_hbm, c * CH * 2048, [[2048, 128], [128 * 2048, 2], [1, 2048]]),
                         bass.AP(o_stage, 0, [[2 * 2048, 128], [2048, 2], [1, 2048]]),
                         waits=[ostage_done])
                t2 = dma("st", "sy",
                         bass.AP(qhat_hbm, c * CH, [[TOK, 128], [128 * TOK, KC], [1, CH]]),
                         bass.AP(qhat_stage, 0, [[2048, 128], [CH, KC], [1, CH]]))
                T.pend["dv"].append(t1)
                T.pend["dv"].append(t2)
            s_final = ("dv", T.cur("dv"))
            sdma = dma("st", "sy",
                bass.AP(cc3_in, 0, [[DV, 128], [2 * 128 * DV, H], [128 * DV, 2], [1, DV]]),
                bass.AP(S32, 0, [[H * 2 * DV, 128], [2 * DV, H], [DV, 2], [1, DV]]),
                waits=[s_final])
            T.emit("gp", lambda e: e.collective_compute(
                "AllGather", ALU.bypass,
                replica_groups=[[0, 1], [2, 3], [4, 5], [6, 7]],
                ins=[cc3_in.ap().opt()], outs=[cc3_out.ap().opt()]),
                waits=[sdma], incs=(("cc", 1),))
            ag3_done = ("cc", T.cur("cc"))
            T.barrier()

    with ExitStack() as pb_stk:
        def bsb(name, cols, dtype=BF16):
            return pb_stk.enter_context(nc.sbuf_tensor(name, [128, cols], dtype))
        Wg_sb = bsb("Wg_sb", KC * 2048)
        Wo_sb = bsb("Wo_sb", 16 * D)
        S_seed = bsb("S_seed", H * 2 * DV)
        o_in = [bsb(f"o_in{p}", 2048) for p in range(4)]
        qhat_in = [bsb(f"qhat_in{p}", 2048) for p in range(3)]
        o_f32 = bsb("o_f32", 2048, F32)
        g_bf = [bsb(f"g_bf{p}", 2048) for p in range(4)]
        ogT = bsb("ogT", 16 * 128)
        xln_tmB = [bsb(f"xln_tmB{p}", D) for p in range(2)]
        xlnTB = bsb("xlnTB", D)

        _wg_tok = dma("gld", "gp", Wg_sb[:, :],
            bass.AP(dwg, 0, [[2048, 128], [128 * 2048, KC], [1, 2048]]))
        _wo_tok = dma("gld", "gp", Wo_sb[:, :],
            bass.AP(dwo, 0, [[D, 128], [128 * D, 16], [1, D]]))
        wgwo_tok = _wo_tok
        for vc in range(16):
            T.emit("dv", lambda e, vc=vc: e.tensor_scalar_mul(
                Wo_sb[:, vc * D:(vc + 1) * D], Wo_sb[:, vc * D:(vc + 1) * D],
                gnwt_sb[:, vc:vc + 1]),
                waits=[_wg_tok, _wo_tok, w_gnw], incs=(("dv", 1),))
        wo_ready = ("dv", T.cur("dv"))
        sc = dma("gld", "gp", S_seed[:, :],
                 bass.AP(cc3_out, 0, [[DV, 128], [2 * 128 * DV, H], [128 * DV, 2], [1, DV]]),
                 waits=[ag3_done])
        for i in range(4):
            T.emit("dv", lambda e, i=i: e.tensor_scalar_mul(
                S_seed[:, i * 2 * DV:(i + 1) * 2 * DV],
                S_seed[:, i * 2 * DV:(i + 1) * 2 * DV], flag_sb[:, 0:1]),
                waits=[sc, w_flag], incs=(("dv", 1),))
        seed_ready = ("dv", T.cur("dv"))

        b1_toks = {}
        b1_pe = {}
        qh_tok = None

        def phb1(t):
            nonlocal qh_tok
            cchunk = t // 2
            cpar = cchunk % 3
            par = t % 4
            if t % 2 == 0:
                qh_tok = dma("ld", "sy",
                             bass.AP(qhat_in[cpar], 0, [[2048, 128], [CH, KC], [1, CH]]),
                             bass.AP(qhat_hbm, cchunk * CH,
                                     [[TOK, 128], [128 * TOK, KC], [1, CH]]))
            oi_tok = dma("ld", "sy", o_in[par][:, :],
                         bass.AP(o_hbm, t * 128 * 2048, [[2048, 128], [1, 2048]]))
            guard = [b1_pe[t - 2]] if (t - 2) in b1_pe else []
            a_tok = T.emit("ac", lambda e, t=t, bpar=t % 2: e.activation(
                xln_tmB[bpar][:, :], x_sb[:, t * D:(t + 1) * D], AF.Identity,
                bias=nmr_ln[:, NT + t:NT + t + 1], scale=r_ln[:, NT + t:NT + t + 1]),
                waits=guard, incs=(("ac", 1),))[0]
            pe_tok = T.emit("pe", lambda e, bpar=t % 2: [
                e.transpose(pt0[:, kc * 128:(kc + 1) * 128],
                            xln_tmB[bpar][:, kc * 128:(kc + 1) * 128], ident_bf[:, :])
                for kc in range(KC)][-1],
                waits=[a_tok] + bgw("pt0"), incs=(("pe", 1),))[0]
            xT_tok = T.emit("dv", lambda e: e.tensor_copy(xlnTB[:, :], pt0[:, :]),
                            waits=[pe_tok], incs=(("dv", 1),))[0]
            bg["pt0"] = xT_tok
            b1_pe[t] = pe_tok
            g_toks = []
            for jb in range(4):
                bank = ps[2 + jb % 2]
                bk = f"ps{2 + jb % 2}"
                def fng(e, jb=jb, bank=bank):
                    last = None
                    for kc in range(KC):
                        last = e.matmul(bank[:, :], xlnTB[:, kc * 128:(kc + 1) * 128],
                                 Wg_sb[:, kc * 2048 + jb * 512: kc * 2048 + (jb + 1) * 512],
                                 start=(kc == 0), stop=(kc == KC - 1))
                    return last
                gpe = T.emit("pe", fng, waits=[xT_tok, _wg_tok] + bgw(bk),
                             incs=(("pe", 1),))[0]
                gev = T.emit("ac", lambda e, jb=jb, bank=bank, par=par: e.activation(
                    g_bf[par][:, jb * 512:(jb + 1) * 512], bank[:, :], AF.Silu,
                    bias=0.0, scale=1.0),
                    waits=[gpe], incs=(("ac", 1),))[0]
                g_toks.append(gev)
                bg[bk] = gev
            b1_toks[t] = (qh_tok, oi_tok, g_toks)

        def phb2(t):
            par = t % 4
            qh_t, oi_tok, g_toks = b1_toks.pop(t)
            oadd_last = None
            for h in range(H):
                bank = ps[4 + (h % 2)]
                bk = f"ps{4 + (h % 2)}"
                def fnc(e, h=h, t=t, bank=bank, cpar=(t // 2) % 3):
                    last = None
                    for dc in range(2):
                        last = e.matmul(bank[:, :],
                                 qhat_in[cpar][:, (2 * h + dc) * CH + (t % 2) * 128:
                                         (2 * h + dc) * CH + (t % 2) * 128 + 128],
                                 S_seed[:, (h * 2 + dc) * DV:(h * 2 + dc + 1) * DV],
                                 start=(dc == 0), stop=(dc == 1))
                    return last
                cpe = T.emit("pe", fnc,
                             waits=[seed_ready, qh_t] + bgw(bk),
                             incs=(("pe", 1),))[0]
                oadd_last = T.emit("dv", lambda e, h=h, bank=bank, par=par: e.scalar_tensor_tensor(
                    o_f32[:, h * DV:(h + 1) * DV], bank[:, :], 1.0,
                    o_in[par][:, h * DV:(h + 1) * DV], ALU.mult, ALU.add),
                    waits=[cpe, oi_tok], incs=(("dv", 1),))[0]
                bg[bk] = oadd_last
                T.emit("ac", lambda e, h=h: e.activation(
                    junk[:, 0:DV], o_f32[:, h * DV:(h + 1) * DV], AF.Square,
                    bias=0.0, scale=1.0, accum_out=ssq[:, h:h + 1]),
                    waits=[oadd_last], incs=(("ac", 1),))
            sq_done = ("ac", T.cur("ac"))
            T.emit("dv", lambda e: e.tensor_scalar(ssq[:, 4:8], ssq[:, 0:4], 1.0 / DV, EPS,
                                                   ALU.mult, ALU.add),
                   waits=[sq_done], incs=(("dv", 1),))
            T.emit("dv", lambda e: e.reciprocal(ssq[:, 0:4], ssq[:, 4:8]), incs=(("dv", 1),))
            rcp = ("dv", T.cur("dv"))
            T.emit("ac", lambda e: e.activation(rh_sb[:, 0:4], ssq[:, 0:4], AF.Sqrt,
                                                bias=0.0, scale=1.0),
                   waits=[rcp], incs=(("ac", 1),))
            r0t = ("ac", T.cur("ac"))
            T.emit("dv", lambda e: e.tensor_tensor(ssq[:, 0:4], rh_sb[:, 0:4], rh_sb[:, 0:4],
                                                   ALU.mult),
                   waits=[r0t], incs=(("dv", 1),))
            T.emit("dv", lambda e: e.tensor_tensor(ssq[:, 0:4], ssq[:, 0:4], ssq[:, 4:8],
                                                   ALU.mult), incs=(("dv", 1),))
            T.emit("dv", lambda e: e.tensor_scalar(ssq[:, 0:4], ssq[:, 0:4], -0.5, 1.5,
                                                   ALU.mult, ALU.add), incs=(("dv", 1),))
            T.emit("dv", lambda e: e.tensor_tensor(rh_sb[:, 0:4], rh_sb[:, 0:4], ssq[:, 0:4],
                                                   ALU.mult), incs=(("dv", 1),))
            rh_done = ("dv", T.cur("dv"))
            T.pend["ac"].append(rh_done)
            og_last = None
            for h in range(H):
                og_last = T.emit("dv", lambda e, h=h, par=par: e.scalar_tensor_tensor(
                    g_bf[par][:, h * DV:(h + 1) * DV], o_f32[:, h * DV:(h + 1) * DV],
                    rh_sb[:, 0 + h:1 + h], g_bf[par][:, h * DV:(h + 1) * DV],
                    ALU.mult, ALU.mult),
                    waits=[rh_done, g_toks[h]], incs=(("dv", 1),))[0]
            og_last_gp = og_last
            t_last = None
            for half in range(2):
                ppe = T.emit("pe", lambda e, half=half, par=par: [
                    e.transpose(pt1[:, i * 128:(i + 1) * 128],
                                g_bf[par][:, half * 1024 + i * 128: half * 1024 + (i + 1) * 128],
                                ident_bf[:, :]) for i in range(8)][-1],
                    waits=[og_last, og_last_gp] + bgw("pt1"), incs=(("pe", 1),))[0]
                t_last = T.emit("dv", lambda e, half=half: e.tensor_copy(
                    ogT[:, half * 1024:(half + 1) * 1024], pt1[:, :]),
                    waits=[ppe], incs=(("dv", 1),))[0]
                bg["pt1"] = t_last
            for jb in range(2):
                bank = ps[jb]
                def fny(e, jb=jb, bank=bank):
                    last = None
                    for vc in range(16):
                        last = e.matmul(bank[:, :], ogT[:, vc * 128:(vc + 1) * 128],
                                 Wo_sb[:, vc * D + jb * 512: vc * D + (jb + 1) * 512],
                                 start=(vc == 0), stop=(vc == 15))
                    return last
                ype = T.emit("pe", fny, waits=[t_last, wo_ready] + bgw(f"ps{jb}"),
                             incs=(("pe", 1),))[0]
                yev = T.emit("dv", lambda e, t=t, jb=jb, bank=bank: e.scalar_tensor_tensor(
                    x_sb[:, t * D + jb * 512: t * D + (jb + 1) * 512], bank[:, :], 1.0,
                    x_sb[:, t * D + jb * 512: t * D + (jb + 1) * 512], ALU.mult, ALU.add),
                    waits=[ype], incs=(("dv", 1),))[0]
                bg[f"ps{jb}"] = yev
            ln_stats(t, scratch=ogT)
            T.pend["sy"].append(("dv", T.cur("dv")))
            T.pend["sy"].append(("pe", T.cur("pe")))

        phb1(0)
        phb1(1)
        phb1(2)
        for t in range(NT):
            if t + 3 < NT:
                phb1(t + 3)
            phb2(t)
        T.barrier()

    ln3_r = rsqrt_chain(2, D)

    with ExitStack() as ffn2_stk:
        Mh2 = ffn2_stk.enter_context(nc.sbuf_tensor("Mh2", [128, KC * D], BF16))
        Ml2 = ffn2_stk.enter_context(nc.sbuf_tensor("Ml2", [128, KC * D], BF16))
        xln_tm2 = [ffn2_stk.enter_context(nc.sbuf_tensor(f"xln_tm2{p}", [128, D], BF16))
                   for p in range(2)]
        xlnT2b = [ffn2_stk.enter_context(nc.sbuf_tensor(f"xlnT2b{p}", [128, D], BF16))
                  for p in range(2)]
        brow2 = ffn2_stk.enter_context(nc.sbuf_tensor("brow2", [128, D], BF16))
        brow2_tok = dma("gld", "gp", brow2[0:1, :], dbias[:, :])
        m2_ld = []
        for i, dst in enumerate((Mh2, Ml2)):
            for r in range(NCORES):
                m2_ld.append(dma("ld", "sy",
                    bass.AP(dst, 128 * r, [[KC * D, 128], [D, KC], [1, 128]]),
                    bass.AP(cc2_out, (r * 2 + i) * D * 128,
                            [[128, 128], [128 * 128, KC], [1, 128]])))
        ffn_pass(2, Mh2, Ml2, m2_ld, xln_tm2, xlnT2b, ln3_r, do_stats=False, do_out=True,
                 brow=brow2, brow_tok=brow2_tok)

    T.emit("sy", lambda e: e.nop(), waits=list(out_toks), incs=(("gp", 1),))

    with nc.Block() as block:
        T.realize(block, semh)
    stk.close()
    return nc


_NC_CACHE = None


def _get_nc():
    global _NC_CACHE
    if _NC_CACHE is None:
        _NC_CACHE = build_nc()
    return _NC_CACHE


def kernel(L0_attn_x, L0_attn_Wq, L0_attn_Wk, L0_attn_Wv, L0_attn_Wg, L0_attn_Wo,
           L0_attn_gn_weight, L0_ffn_w_freq_real, L0_ffn_w_freq_imag, L0_ffn_bias):
    c = host_constants()
    x = np.ascontiguousarray(np.asarray(L0_attn_x, dtype=np.float32))
    f32 = lambda a: np.ascontiguousarray(np.asarray(a, dtype=np.float32))
    gnw = np.asarray(L0_attn_gn_weight, dtype=np.float32)
    vc_idx = (np.arange(16)[None, :] * 128 + np.arange(128)[:, None]) % 512
    gnwt = np.ascontiguousarray(gnw[vc_idx].astype(np.float32))
    shared = {
        "wq": f32(L0_attn_Wq), "wk": f32(L0_attn_Wk), "wv": f32(L0_attn_Wv),
        "wg": f32(L0_attn_Wg), "wo": f32(L0_attn_Wo),
        "wfr": f32(L0_ffn_w_freq_real), "wfi": f32(L0_ffn_w_freq_imag),
        "bias": f32(L0_ffn_bias).reshape(1, D),
        "gnwt": gnwt,
        "dts": c["dts"].astype(np.float32),
        "cqb": c["cqb"].astype(np.float32),
        "ckb": c["ckb"].astype(np.float32),
        "identc": c["ident"].astype(np.float32),
    }
    in_maps = []
    for core in range(NCORES):
        b, half = core // 2, core % 2
        pos = slice(half * TOK, (half + 1) * TOK)
        m = dict(shared)
        m["x"] = np.ascontiguousarray(x[b, pos, :])
        m["frs"] = np.ascontiguousarray(c["C"][:, core * 128:(core + 1) * 128].astype(np.float32))
        m["fis"] = np.ascontiguousarray(c["S"][:, core * 128:(core + 1) * 128].astype(np.float32))
        m["fisn"] = np.ascontiguousarray((-c["S"][:, core * 128:(core + 1) * 128]).astype(np.float32))
        m["cosf"] = np.ascontiguousarray(c["cos"][pos, :].T.astype(np.float32))
        m["sinf"] = np.ascontiguousarray(c["sin"][pos, :].T.astype(np.float32))
        m["flagcol"] = np.full((128, 1), float(half), dtype=np.float32)
        in_maps.append(m)
    nc = _get_nc()
    res = run_bass_kernel_spmd(nc, in_maps, core_ids=list(range(NCORES)))
    out = np.empty((4, 4096, D), dtype=np.float32)
    for core in range(NCORES):
        b, half = core // 2, core % 2
        out[b, half * TOK:(half + 1) * TOK, :] = res.results[core]["y"]
    return out


# version 14 (final)
# speedup vs baseline: 1.0331x; 1.0319x over previous
import numpy as np
from contextlib import ExitStack
import concourse.bass as bass
import concourse.mybir as mybir
from concourse.bass_utils import run_bass_kernel_spmd

F32 = mybir.dt.float32
BF16 = mybir.dt.bfloat16
AF = mybir.ActivationFunctionType
ALU = mybir.AluOpType
AXX = mybir.AxisListType.X

NCORES = 8
D = 1024
TOK = 2048
NT = 16
H = 4
DK = 256
DV = 512
CH = 256
NCH = TOK // CH
EPS = 1e-5
KC = 8

ENGS = ("pe", "dv", "ac", "gp", "sy")


class Tape:

    def __init__(self):
        self.rows = []
        self.cnt = {}
        self.pend = {e: [] for e in ENGS}

    def cur(self, sem):
        return self.cnt.get(sem, 0)

    def emit(self, eng, fn, waits=(), incs=()):
        w = self.pend[eng] + list(waits)
        self.pend[eng] = []
        if eng in ("dv", "ac", "gp") and self.cur(eng) > 0:
            w.append((eng, self.cur(eng)))
        for s, v in w:
            assert v <= self.cur(s), f"wait {s}>={v} not satisfiable (cur {self.cur(s)})"
        self.rows.append((eng, fn, w, list(incs)))
        for s, n in incs:
            self.cnt[s] = self.cur(s) + n
        return [(s, self.cur(s)) for s, _ in incs]

    def barrier(self):
        dma_waits = {"sy": [], "gp": []}
        for s, v in self.cnt.items():
            if s.startswith("syd") and v > 0:
                dma_waits["sy"].append((s, v))
            elif s.startswith("gpd") and v > 0:
                dma_waits["gp"].append((s, v))
        for e in ENGS:
            self.emit(e, lambda en: en.nop(), waits=dma_waits.get(e, ()),
                      incs=(("bar", 1),))
        v = self.cur("bar")
        for e in ENGS:
            self.pend[e].append(("bar", v))

    def realize(self, block, semh):
        for eng_name, deco in (("pe", block.tensor), ("dv", block.vector),
                               ("ac", block.scalar), ("gp", block.gpsimd),
                               ("sy", block.sync)):
            def mk(eng_name=eng_name):
                def emitter(e):
                    seen = {}
                    for g, fn, waits, incs in self.rows:
                        if g != eng_name:
                            continue
                        for s, v in waits:
                            if seen.get(s, 0) < v:
                                e.wait_ge(semh[s], v)
                                seen[s] = v
                        inst = fn(e)
                        for s, n in incs:
                            inst.then_inc(semh[s], n)
                return emitter
            deco(mk())


def host_constants():
    c = {}
    n = np.arange(D)
    ang = 2.0 * np.pi * np.outer(n, n) / D
    c["C"] = np.cos(ang)
    c["S"] = -np.sin(ang)
    inv_freq = 1.0 / (10000.0 ** (np.arange(0, DK, 2) / DK))
    t = np.arange(4096)
    fr = np.outer(t, inv_freq)
    c["cos"] = np.cos(fr)
    c["sin"] = np.sin(fr)
    gamma = 1.0 - np.exp2(-5.0 - np.arange(H))
    c["gamma"] = gamma
    idx = np.arange(CH)
    dts = np.zeros((128, H * 2 * CH))
    for h in range(H):
        g = gamma[h]
        DT = np.where(idx[:, None] <= idx[None, :],
                      g ** (idx[None, :] - idx[:, None]), 0.0) / 16.0
        for jt in range(2):
            dts[:, (h * 2 + jt) * CH:(h * 2 + jt + 1) * CH] = DT[jt * 128:(jt + 1) * 128, :]
    c["dts"] = dts
    cqb = np.zeros((128, H * CH))
    ckb = np.zeros((128, H * CH))
    for h in range(H):
        g = gamma[h]
        cqb[:, h * CH:(h + 1) * CH] = (g ** (idx + 1.0) / 16.0)[None, :]
        ckb[:, h * CH:(h + 1) * CH] = (g ** (CH - 1.0 - idx))[None, :]
    c["cqb"] = cqb
    c["ckb"] = ckb
    c["ident"] = np.eye(128)
    return c


def build_nc(debug=False):
    nc = bass.Bass("TRN2", target_bir_lowering=False)
    dbg_kind = "ExternalOutput" if debug else "Internal"

    dx = nc.dram_tensor("x", [TOK, D], F32, kind="ExternalInput")
    dwq = nc.dram_tensor("wq", [D, D], F32, kind="ExternalInput")
    dwk = nc.dram_tensor("wk", [D, D], F32, kind="ExternalInput")
    dwv = nc.dram_tensor("wv", [D, 2048], F32, kind="ExternalInput")
    dwg = nc.dram_tensor("wg", [D, 2048], F32, kind="ExternalInput")
    dwo = nc.dram_tensor("wo", [2048, D], F32, kind="ExternalInput")
    dwfr = nc.dram_tensor("wfr", [D, D], F32, kind="ExternalInput")
    dwfi = nc.dram_tensor("wfi", [D, D], F32, kind="ExternalInput")
    dbias = nc.dram_tensor("bias", [1, D], F32, kind="ExternalInput")
    dgnwt = nc.dram_tensor("gnwt", [128, 16], F32, kind="ExternalInput")
    dfrs = nc.dram_tensor("frs", [D, 128], F32, kind="ExternalInput")
    dfis = nc.dram_tensor("fis", [D, 128], F32, kind="ExternalInput")
    dfisn = nc.dram_tensor("fisn", [D, 128], F32, kind="ExternalInput")
    dcos = nc.dram_tensor("cosf", [128, TOK], F32, kind="ExternalInput")
    dsin = nc.dram_tensor("sinf", [128, TOK], F32, kind="ExternalInput")
    ddts = nc.dram_tensor("dts", [128, H * 2 * CH], F32, kind="ExternalInput")
    dcqb = nc.dram_tensor("cqb", [128, H * CH], F32, kind="ExternalInput")
    dckb = nc.dram_tensor("ckb", [128, H * CH], F32, kind="ExternalInput")
    dident = nc.dram_tensor("identc", [128, 128], F32, kind="ExternalInput")
    dflag = nc.dram_tensor("flagcol", [128, 1], F32, kind="ExternalInput")
    dy = nc.dram_tensor("y", [TOK, D], F32, kind="ExternalOutput")

    o_hbm = nc.dram_tensor("o_hbm", [TOK, 2048], BF16, kind=dbg_kind)
    qhat_hbm = nc.dram_tensor("qhat_hbm", [D, TOK], BF16, kind=dbg_kind)
    cc1a_in = nc.dram_tensor("cc1a_in", [2, D, 128], BF16)
    cc1a_out = nc.dram_tensor("cc1a_out", [NCORES * 2, D, 128], BF16, addr_space="Shared")
    cc1b_in = nc.dram_tensor("cc1b_in", [2, D, 128], BF16)
    cc1b_out = nc.dram_tensor("cc1b_out", [NCORES * 2, D, 128], BF16, addr_space="Shared")
    cc2_in = nc.dram_tensor("cc2_in", [2, D, 128], BF16)
    cc2_out = nc.dram_tensor("cc2_out", [NCORES * 2, D, 128], BF16, addr_space="Shared")
    cc3_in = nc.dram_tensor("cc3_in", [H * 2 * 128, DV], F32, kind=dbg_kind)
    dbg_q = nc.dram_tensor("dbg_q", [128, 4096], BF16, kind=dbg_kind)
    cc3_out = nc.dram_tensor("cc3_out", [2 * H * 2 * 128, DV], F32)

    gamma = 1.0 - np.exp2(-5.0 - np.arange(H))
    gC = [float(gamma[h] ** CH) for h in range(H)]
    ghat = [[float(gamma[h] ** (CH * cc)) for cc in range(NCH)] for h in range(H)]

    T = Tape()
    stk = ExitStack()
    bg = {}

    def bgw(*keys):
        return [bg[k] for k in keys if k in bg]

    def sb(name, cols, dtype):
        return stk.enter_context(nc.sbuf_tensor(name, [128, cols], dtype))

    semnames = ([f"syd{i}" for i in range(12)] + [f"gpd{i}" for i in range(6)]
                + ["pe", "dv", "ac", "gp", "cc", "bar"])
    semh = {s: stk.enter_context(nc.semaphore(s)) for s in semnames}

    x_sb = sb("x_sb", NT * D, F32)
    cqb_sb = sb("cqb_sb", H * CH, BF16)
    ckb_sb = sb("ckb_sb", H * CH, BF16)
    ident_bf = sb("ident_bf", 128, BF16)
    gnwt_sb = sb("gnwt_sb", 16, F32)
    flag_sb = sb("flag_sb", 1, F32)
    ones_row = sb("ones_row", 128, BF16)
    junk = sb("junk", D, BF16)
    stats = sb("stats", 64, F32)
    r_ln = sb("r_ln", 3 * NT, F32)
    nmr_ln = sb("nmr_ln", 3 * NT, F32)
    ssq = sb("ssq", 8, F32)
    rh_sb = sb("rh_sb", 4, F32)

    ps = [stk.enter_context(nc.psum_tensor(f"ps{i}", [128, 512], F32)) for i in range(6)]
    pt0 = stk.enter_context(nc.psum_tensor("pt0", [128, 1024], BF16))
    pt1 = stk.enter_context(nc.psum_tensor("pt1", [128, 1024], BF16))

    _dma_pool = {"sy": [f"syd{i}" for i in range(12)],
                 "gp": [f"gpd{i}" for i in range(6)]}
    _dma_rr = {"sy": 0, "gp": 0}

    def dma(_legacy, eng, out, in_, waits=()):
        pool = _dma_pool[eng]
        s = pool[_dma_rr[eng] % len(pool)]
        _dma_rr[eng] += 1
        prev = T.cur(s)
        return T.emit(eng, lambda e, o=out, i=in_: e.dma_start(out=o, in_=i),
                      waits=list(waits) + [(s, prev)], incs=((s, 16),))[0]

    w_x = dma("ld", "sy", x_sb[:, :],
              bass.AP(dx, 0, [[D, 128], [128 * D, NT], [1, D]]))
    w_cqb = dma("gld", "gp", cqb_sb[:, :], dcqb[:, :])
    w_ckb = dma("gld", "gp", ckb_sb[:, :], dckb[:, :])
    w_id = dma("gld", "gp", ident_bf[:, :], dident[:, :])
    w_gnw = dma("ld", "sy", gnwt_sb[:, :], dgnwt[:, :])
    w_flag = dma("ld", "sy", flag_sb[:, :], dflag[:, :])
    T.emit("dv", lambda e: e.memset(ones_row[0:1, :], 0.5), incs=(("dv", 1),))
    ones_tok = ("dv", T.cur("dv"))

    def ln_stats(tile, waits_dv=(), waits_ac=None, scratch=None, eng="dv"):
        scr = junk if scratch is None else scratch
        T.emit(eng, lambda e, t=tile: e.reduce_sum(
            stats[:, t:t + 1], x_sb[:, t * D:(t + 1) * D], axis=AXX),
            waits=waits_dv, incs=((eng, 1),))
        T.emit(eng, lambda e, t=tile, scr=scr: e.tensor_tensor(
            scr[:, 0:D], x_sb[:, t * D:(t + 1) * D], x_sb[:, t * D:(t + 1) * D], ALU.mult),
            incs=((eng, 1),))
        T.emit(eng, lambda e, t=tile, scr=scr: e.reduce_sum(
            stats[:, 16 + t:17 + t], scr[:, 0:D], axis=AXX),
            incs=((eng, 1),))

    def rsqrt_chain(slot, nfeat, waits=()):
        a = stats[:, 32:48]
        b = stats[:, 48:64]
        sl = slice(slot * NT, slot * NT + NT)
        inv = 1.0 / nfeat
        T.emit("dv", lambda e: e.tensor_scalar_mul(a, stats[:, 0:16], inv),
               waits=waits, incs=(("dv", 1),))
        T.emit("dv", lambda e: e.tensor_scalar_mul(b, stats[:, 16:32], inv), incs=(("dv", 1),))
        T.emit("dv", lambda e: e.tensor_tensor(stats[:, 0:16], a, a, ALU.mult), incs=(("dv", 1),))
        T.emit("dv", lambda e: e.tensor_tensor(b, b, stats[:, 0:16], ALU.subtract), incs=(("dv", 1),))
        T.emit("dv", lambda e: e.tensor_scalar_add(b, b, EPS), incs=(("dv", 1),))
        T.emit("dv", lambda e: e.reciprocal(stats[:, 0:16], b), incs=(("dv", 1),))
        tok_dv = ("dv", T.cur("dv"))
        T.emit("ac", lambda e: e.activation(r_ln[:, sl], stats[:, 0:16], AF.Sqrt,
                                            bias=0.0, scale=1.0),
               waits=[tok_dv], incs=(("ac", 1),))
        tok_ac = ("ac", T.cur("ac"))
        T.emit("dv", lambda e: e.tensor_tensor(stats[:, 0:16], r_ln[:, sl], r_ln[:, sl], ALU.mult),
               waits=[tok_ac], incs=(("dv", 1),))
        T.emit("dv", lambda e: e.tensor_tensor(stats[:, 0:16], stats[:, 0:16], b, ALU.mult),
               incs=(("dv", 1),))
        T.emit("dv", lambda e: e.tensor_scalar(stats[:, 0:16], stats[:, 0:16], -0.5, 1.5,
                                               ALU.mult, ALU.add), incs=(("dv", 1),))
        T.emit("dv", lambda e: e.tensor_tensor(r_ln[:, sl], r_ln[:, sl], stats[:, 0:16], ALU.mult),
               incs=(("dv", 1),))
        T.emit("dv", lambda e: e.tensor_tensor(stats[:, 0:16], a, r_ln[:, sl], ALU.mult),
               incs=(("dv", 1),))
        T.emit("dv", lambda e: e.tensor_scalar_mul(nmr_ln[:, sl], stats[:, 0:16], -1.0),
               incs=(("dv", 1),))
        return ("dv", T.cur("dv"))

    for t in range(NT):
        ln_stats(t, waits_dv=[w_x], waits_ac=[w_x])
    ln1_stats = [("dv", T.cur("dv")), ("ac", T.cur("ac"))]

    with ExitStack() as form_stk:
        def fsb(name, cols, dtype=BF16):
            return form_stk.enter_context(nc.sbuf_tensor(name, [128, cols], dtype))
        frsh = fsb("frsh", 1024); frsl = fsb("frsl", 1024)
        fish = fsb("fish", 1024); fisl = fsb("fisl", 1024)
        fisnh = fsb("fisnh", 1024); fisnl = fsb("fisnl", 1024)
        ath = fsb("ath", 1024); atl = fsb("atl", 1024)
        bth = fsb("bth", 1024); btl = fsb("btl", 1024)
        mhb = fsb("mhb", 1024); mlb = fsb("mlb", 1024)
        stg2 = fsb("stg2", 1024, F32)

        def slice_ap(dt_):
            return bass.AP(dt_, 0, [[128, 128], [128 * 128, KC], [1, 128]])
        for drm, hh, ll in ((dfrs, frsh, frsl), (dfis, fish, fisl), (dfisn, fisnh, fisnl)):
            wtok = dma("gld", "gp", hh[:, :], slice_ap(drm))
            wtok2 = dma("ld", "sy", stg2[:, :], slice_ap(drm))
            dv_tok = T.emit("dv", lambda e, hh=hh, ll=ll: e.tensor_tensor(
                ll[:, :], stg2[:, :], hh[:, :], ALU.subtract),
                waits=[wtok, wtok2], incs=(("dv", 1),))[0]
            T.pend["sy"].append(dv_tok)
        fsplit_done = [("dv", T.cur("dv"))]

        with ExitStack() as wsplit_stk:
            def wsb(name):
                return wsplit_stk.enter_context(nc.sbuf_tensor(name, [128, KC * D], BF16))
            wrh, wrl = wsb("wrh"), wsb("wrl")
            wih, wil = wsb("wih"), wsb("wil")
            stgc = wsplit_stk.enter_context(nc.sbuf_tensor("stgc", [128, D], F32))
            for dw, whh, wll in ((dwfr, wrh, wrl), (dwfi, wih, wil)):
                wh_tok = dma("gld", "gp", whh[:, :],
                             bass.AP(dw, 0, [[D, 128], [128 * D, KC], [1, D]]))
                for kc in range(KC):
                    s_tok = dma("ld", "sy", stgc[:, :],
                                bass.AP(dw, kc * 128 * D, [[D, 128], [1, D]]))
                    dv_tok = T.emit("dv", lambda e, kc=kc, wll=wll, whh=whh: e.tensor_tensor(
                        wll[:, kc * D:(kc + 1) * D], stgc[:, :],
                        whh[:, kc * D:(kc + 1) * D], ALU.subtract),
                        waits=[s_tok, wh_tok], incs=(("dv", 1),))[0]
                    T.pend["sy"].append(dv_tok)
            splits_done = fsplit_done + [("dv", T.cur("dv"))]

            def mform_stage(dst_h, dst_l, opnds, scale=None):
                for mt in range(KC):
                    bank = ps[mt % 4]
                    bkey = f"ps{mt % 4}"
                    def fn(e, mt=mt, bank=bank, opnds=tuple(opnds)):
                        n = len(opnds) * KC
                        i = 0
                        last = None
                        for wH, cH in opnds:
                            for kc in range(KC):
                                last = e.matmul(bank[:, 0:128],
                                         wH[:, kc * D + mt * 128: kc * D + mt * 128 + 128],
                                         cH[:, kc * 128:(kc + 1) * 128],
                                         start=(i == 0), stop=(i == n - 1))
                                i += 1
                        return last
                    pe_tok = T.emit("pe", fn, waits=list(splits_done) + bgw(bkey),
                                    incs=(("pe", 1),))[0]
                    if scale is None:
                        T.emit("dv", lambda e, bank=bank, mt=mt, dst_h=dst_h: e.tensor_copy(
                            dst_h[:, mt * 128:(mt + 1) * 128], bank[:, 0:128]),
                            waits=[pe_tok], incs=(("dv", 1),))
                        d2 = T.emit("dv", lambda e, bank=bank, mt=mt, dst_h=dst_h, dst_l=dst_l:
                                    e.scalar_tensor_tensor(
                                        dst_l[:, mt * 128:(mt + 1) * 128], bank[:, 0:128], 1.0,
                                        dst_h[:, mt * 128:(mt + 1) * 128], ALU.mult, ALU.subtract),
                                    incs=(("dv", 1),))[0]
                    else:
                        T.emit("dv", lambda e, bank=bank, mt=mt, dst_h=dst_h: e.tensor_scalar_mul(
                            dst_h[:, mt * 128:(mt + 1) * 128], bank[:, 0:128], scale),
                            waits=[pe_tok], incs=(("dv", 1),))
                        d2 = T.emit("dv", lambda e, bank=bank, mt=mt, dst_h=dst_h, dst_l=dst_l:
                                    e.scalar_tensor_tensor(
                                        dst_l[:, mt * 128:(mt + 1) * 128], bank[:, 0:128], scale,
                                        dst_h[:, mt * 128:(mt + 1) * 128], ALU.mult, ALU.subtract),
                                    incs=(("dv", 1),))[0]
                    bg[bkey] = d2
                return ("dv", T.cur("dv"))

            at_done = mform_stage(ath, atl, [(wrh, frsh), (wrl, frsh), (wrh, frsl),
                                             (wih, fisnh), (wil, fisnh), (wih, fisnl)])
            ag1a_st = [dma("st", "sy",
                    bass.AP(cc1a_in, i * D * 128, [[128, 128], [128 * 128, KC], [1, 128]]),
                    srct[:, :], waits=[at_done] if i == 0 else ())
                for i, srct in enumerate((ath, atl))]
            T.emit("gp", lambda e: e.collective_compute(
                "AllGather", ALU.bypass, replica_groups=[list(range(NCORES))],
                ins=[cc1a_in.ap().opt()], outs=[cc1a_out.ap().opt()]),
                waits=ag1a_st, incs=(("cc", 1),))
            ag1a_done = ("cc", T.cur("cc"))
            bt_done = mform_stage(bth, btl, [(wih, frsh), (wil, frsh), (wih, frsl),
                                             (wrh, fish), (wrl, fish), (wrh, fisl)])
            ag1b_st = [dma("st", "sy",
                    bass.AP(cc1b_in, i * D * 128, [[128, 128], [128 * 128, KC], [1, 128]]),
                    srct[:, :], waits=[bt_done] if i == 0 else ())
                for i, srct in enumerate((bth, btl))]
            T.emit("gp", lambda e: e.collective_compute(
                "AllGather", ALU.bypass, replica_groups=[list(range(NCORES))],
                ins=[cc1b_in.ap().opt()], outs=[cc1b_out.ap().opt()]),
                waits=ag1b_st, incs=(("cc", 1),))
            ag1b_done = ("cc", T.cur("cc"))
            T.barrier()

        atfh = fsb("atfh", KC * D); atfl = fsb("atfl", KC * D)
        btfh = fsb("btfh", KC * D); btfl = fsb("btfl", KC * D)
        agload = []
        for ti, dst, ccout, agdone in ((0, atfh, cc1a_out, ag1a_done),
                                       (1, atfl, cc1a_out, ag1a_done),
                                       (0, btfh, cc1b_out, ag1b_done),
                                       (1, btfl, cc1b_out, ag1b_done)):
            for r in range(NCORES):
                agload.append(dma("ld", "sy",
                    bass.AP(dst, 128 * r, [[KC * D, 128], [D, KC], [1, 128]]),
                    bass.AP(ccout, (r * 2 + ti) * D * 128,
                            [[128, 128], [128 * 128, KC], [1, 128]]),
                    waits=[agdone]))

        mscale = 0.5 / D
        saved = T.pend["pe"]
        T.pend["pe"] = saved + list(agload)
        mform_stage(mhb, mlb, [(atfh, frsh), (atfl, frsh), (atfh, frsl),
                               (btfh, fish), (btfl, fish), (btfh, fisl)], scale=mscale)
        s2_done = [("dv", T.cur("dv"))]
        ag2_st_toks = [dma("st", "sy",
                bass.AP(cc2_in, i * D * 128, [[128, 128], [128 * 128, KC], [1, 128]]),
                srct[:, :], waits=s2_done if i == 0 else ())
            for i, srct in enumerate((mhb, mlb))]
        T.emit("gp", lambda e: e.collective_compute(
            "AllGather", ALU.bypass, replica_groups=[list(range(NCORES))],
            ins=[cc2_in.ap().opt()], outs=[cc2_out.ap().opt()]),
            waits=ag2_st_toks, incs=(("cc", 1),))
        ag2_done = ("cc", T.cur("cc"))
        T.barrier()

    ln1_r = rsqrt_chain(0, D, waits=ln1_stats)

    out_toks = []

    def ffn_pass(slot, msbh, msbl, m_ld, xln_tm, xlnT, ln_tok, do_stats, do_out,
                 brow, brow_tok):
        sl0 = slot * NT
        prep_toks = {}
        prep_pe = {}

        def prep(t):
            par = t % 2
            guard = [prep_pe[t - 2]] if (t - 2) in prep_pe else []
            a_tok = T.emit("ac", lambda e, t=t, par=par: e.activation(
                xln_tm[par][:, :], x_sb[:, t * D:(t + 1) * D], AF.Identity,
                bias=nmr_ln[:, sl0 + t:sl0 + t + 1], scale=r_ln[:, sl0 + t:sl0 + t + 1]),
                waits=[ln_tok] + guard, incs=(("ac", 1),))[0]
            pe_tok = T.emit("pe", lambda e, par=par: [
                e.transpose(pt0[:, kc * 128:(kc + 1) * 128],
                            xln_tm[par][:, kc * 128:(kc + 1) * 128], ident_bf[:, :])
                for kc in range(KC)][-1],
                waits=[a_tok, w_id] + bgw("pt0"), incs=(("pe", 1),))[0]
            dv_tok = T.emit("dv", lambda e, par=par: e.tensor_copy(xlnT[par][:, :], pt0[:, :]),
                            waits=[pe_tok], incs=(("dv", 1),))[0]
            bg["pt0"] = dv_tok
            prep_toks[t] = dv_tok
            prep_pe[t] = pe_tok

        prep(0)
        for t in range(NT):
            if t + 1 < NT:
                prep(t + 1)
            par = t % 2
            dv_tok = prep_toks.pop(t)
            ev_toks = []
            for jb in range(2):
                bank = ps[(t % 2) * 2 + jb]
                bkey = f"ps{(t % 2) * 2 + jb}"
                def fn(e, jb=jb, bank=bank, par=par):
                    e.matmul(bank[:, :], ones_row[0:1, :],
                             brow[0:1, jb * 512:(jb + 1) * 512], start=True, stop=False)
                    last = None
                    for i, msb in enumerate((msbh, msbl)):
                        for kc in range(KC):
                            last = e.matmul(bank[:, :], xlnT[par][:, kc * 128:(kc + 1) * 128],
                                     msb[:, kc * D + jb * 512: kc * D + (jb + 1) * 512],
                                     start=False, stop=(i == 1 and kc == KC - 1))
                    return last
                pe2 = T.emit("pe", fn,
                             waits=[dv_tok, brow_tok, ones_tok] + list(m_ld) + bgw(bkey),
                             incs=(("pe", 1),))[0]
                ev = T.emit("dv", lambda e, t=t, jb=jb, bank=bank: e.scalar_tensor_tensor(
                    x_sb[:, t * D + jb * 512: t * D + (jb + 1) * 512], bank[:, :], 1.0,
                    x_sb[:, t * D + jb * 512: t * D + (jb + 1) * 512], ALU.mult, ALU.add),
                    waits=[pe2], incs=(("dv", 1),))[0]
                ev_toks.append(ev)
                bg[bkey] = ev
            if do_stats:
                ln_stats(t)
            if do_out:
                out_toks.append(dma("st", "sy",
                    bass.AP(dy, t * 128 * D, [[D, 128], [1, D]]),
                    x_sb[:, t * D:(t + 1) * D], waits=ev_toks))

    with ExitStack() as wqkv_stk:
        def qsb(name, cols):
            return wqkv_stk.enter_context(nc.sbuf_tensor(name, [128, cols], BF16))
        Wq_sb = qsb("Wq_sb", KC * D)
        Wk_sb = qsb("Wk_sb", KC * D)
        Wv_sb = qsb("Wv_sb", KC * 2048)
        _wts = [dma("gld", "gp", dst[:, :],
                    bass.AP(srct, 0, [[width, 128], [128 * width, KC], [1, width]]))
                for dst, srct, width in ((Wq_sb, dwq, D), (Wk_sb, dwk, D), (Wv_sb, dwv, 2048))]
        wqkv_loaded = _wts

        with ExitStack() as ffn1_stk:
            Mh_sb = ffn1_stk.enter_context(nc.sbuf_tensor("Mh_sb", [128, KC * D], BF16))
            Ml_sb = ffn1_stk.enter_context(nc.sbuf_tensor("Ml_sb", [128, KC * D], BF16))
            xln_tm1 = [ffn1_stk.enter_context(nc.sbuf_tensor(f"xln_tm1{p}", [128, D], BF16))
                       for p in range(2)]
            xlnT1 = [ffn1_stk.enter_context(nc.sbuf_tensor(f"xlnT1{p}", [128, D], BF16))
                     for p in range(2)]
            brow1 = ffn1_stk.enter_context(nc.sbuf_tensor("brow1", [128, D], BF16))
            brow1_tok = dma("gld", "gp", brow1[0:1, :], dbias[:, :])
            m_ld = []
            for i, dst in enumerate((Mh_sb, Ml_sb)):
                for r in range(NCORES):
                    m_ld.append(dma("ld", "sy",
                        bass.AP(dst, 128 * r, [[KC * D, 128], [D, KC], [1, 128]]),
                        bass.AP(cc2_out, (r * 2 + i) * D * 128,
                                [[128, 128], [128 * 128, KC], [1, 128]]),
                        waits=[ag2_done]))
            ffn_pass(0, Mh_sb, Ml_sb, m_ld, xln_tm1, xlnT1, ln1_r, do_stats=True,
                     do_out=False, brow=brow1, brow_tok=brow1_tok)
            T.barrier()
        ln2_r = rsqrt_chain(1, D)

        with ExitStack() as pa_stk:
            def asb(name, cols, dtype=BF16):
                return pa_stk.enter_context(nc.sbuf_tensor(name, [128, cols], dtype))
            S32 = asb("S32", H * 2 * DV, F32)
            dts_sb = asb("dts_sb", H * 2 * CH, F32)
            w_dts = dma("ld", "sy", dts_sb[:, :], ddts[:, :])
            S_bf = asb("S_bf", H * 2 * DV)
            xln2_tm = [asb(f"xln2_tm{p}", D) for p in range(2)]
            xlnT2 = asb("xlnT2", KC * CH)
            qh_raw = [asb(f"qh_raw{p}", 512) for p in range(2)]
            kh_raw = [asb(f"kh_raw{p}", 512) for p in range(2)]
            qh_fm = [asb(f"qh_fm{p}", 512) for p in range(2)]
            qt_fm = [asb(f"qt_fm{p}", 512) for p in range(2)]
            kh_fm = [asb(f"kh_fm{p}", 512) for p in range(2)]
            khat_tm = [asb(f"khat_tm{p}", 512) for p in range(2)]
            vh_tm = [asb(f"vh_tm{p}", 2 * DV) for p in range(2)]
            attnT_sb = [asb(f"attnT_sb{p}", 2 * 256) for p in range(2)]
            o_stage = asb("o_stage", 2 * 2048)
            qhat_stage = asb("qhat_stage", 2048)
            cos_c = [asb(f"cos_c{p}", CH) for p in range(2)]
            sin_c = [asb(f"sin_c{p}", CH) for p in range(2)]

            T.emit("dv", lambda e: e.memset(S32[:, :], 0.0), incs=(("dv", 1),))
            T.emit("dv", lambda e: e.memset(S_bf[:, :], 0.0), incs=(("dv", 1),))
            sinit = [("dv", T.cur("dv"))]

            prev_rot_q = [None, None]
            prev_rot_k = [None, None]
            part1_toks = {}

            def part1(c, h, xlnT2_ready):
                par = h % 2
                def qk_proj(W, bank, h=h):
                    def fn(e, W=W, bank=bank, h=h):
                        last = None
                        for dc in range(2):
                            for kc in range(KC):
                                last = e.matmul(bank[:, dc * 256:(dc + 1) * 256],
                                         W[:, kc * D + h * 256 + dc * 128:
                                           kc * D + h * 256 + dc * 128 + 128],
                                         xlnT2[:, kc * CH:(kc + 1) * CH],
                                         start=(kc == 0), stop=(kc == KC - 1))
                        return last
                    return fn
                qpe = T.emit("pe", qk_proj(Wq_sb, ps[0]),
                             waits=[xlnT2_ready] + wqkv_loaded + bgw("ps0"),
                             incs=(("pe", 1),))[0]
                qev = T.emit("ac", lambda e, par=par: e.copy(qh_raw[par][:, :], ps[0][:, :]),
                             waits=[qpe] + ([prev_rot_q[par]] if prev_rot_q[par] else []),
                             incs=(("ac", 1),))[0]
                bg["ps0"] = qev
                kpe = T.emit("pe", qk_proj(Wk_sb, ps[1]), waits=bgw("ps1"),
                             incs=(("pe", 1),))[0]
                kev = T.emit("ac", lambda e, par=par: e.copy(kh_raw[par][:, :], ps[1][:, :]),
                             waits=[kpe] + ([prev_rot_k[par]] if prev_rot_k[par] else []),
                             incs=(("ac", 1),))[0]
                bg["ps1"] = kev
                vev_toks = []
                for jt in range(2):
                    bank = ps[2 + jt]
                    def fnv(e, jt=jt, bank=bank, h=h):
                        last = None
                        for kc in range(KC):
                            last = e.matmul(bank[:, :],
                                     xlnT2[:, kc * CH + jt * 128: kc * CH + jt * 128 + 128],
                                     Wv_sb[:, kc * 2048 + h * DV: kc * 2048 + (h + 1) * DV],
                                     start=(kc == 0), stop=(kc == KC - 1))
                        return last
                    vpe = T.emit("pe", fnv, waits=bgw(f"ps{2 + jt}"),
                                 incs=(("pe", 1),))[0]
                    vev = T.emit("ac", lambda e, jt=jt, bank=bank, par=par: e.copy(
                        vh_tm[par][:, jt * DV:(jt + 1) * DV], bank[:, :]),
                        waits=[vpe], incs=(("ac", 1),))[0]
                    vev_toks.append(vev)
                    bg[f"ps{2 + jt}"] = vev
                part1_toks[(c, h)] = (qev, kev, vev_toks)

            def part2(c, h):
                par = h % 2
                qev, kev, vev_toks = part1_toks.pop((c, h))
                cosl = cos_c[c % 2][:, :]
                sinl = sin_c[c % 2][:, :]
                w_cos, w_sin = rot_toks[c]

                def rot(dst, srcb, wait_tok, eng, so, cosl=cosl, sinl=sinl):
                    T.emit(eng, lambda e, dst=dst, srcb=srcb, cosl=cosl, so=so: e.tensor_tensor(
                        junk[:, so:so + 256], srcb[:, 0:256], cosl, ALU.mult),
                        waits=[wait_tok, w_cos, w_sin], incs=((eng, 1),))
                    T.emit(eng, lambda e, srcb=srcb, sinl=sinl, so=so: e.tensor_tensor(
                        junk[:, so + 256:so + 512], srcb[:, 256:512], sinl, ALU.mult),
                        incs=((eng, 1),))
                    T.emit(eng, lambda e, dst=dst, so=so: e.tensor_tensor(
                        dst[:, 0:256], junk[:, so:so + 256], junk[:, so + 256:so + 512],
                        ALU.subtract),
                        incs=((eng, 1),))
                    T.emit(eng, lambda e, srcb=srcb, cosl=cosl, so=so: e.tensor_tensor(
                        junk[:, so:so + 256], srcb[:, 256:512], cosl, ALU.mult),
                        incs=((eng, 1),))
                    T.emit(eng, lambda e, srcb=srcb, sinl=sinl, so=so: e.tensor_tensor(
                        junk[:, so + 256:so + 512], srcb[:, 0:256], sinl, ALU.mult),
                        incs=((eng, 1),))
                    return T.emit(eng, lambda e, dst=dst, so=so: e.tensor_tensor(
                        dst[:, 256:512], junk[:, so:so + 256], junk[:, so + 256:so + 512],
                        ALU.add),
                        incs=((eng, 1),))[0]

                q_rot = rot(qh_fm[par], qh_raw[par], qev, "dv", 0)
                prev_rot_q[par] = q_rot
                for dc in range(2):
                    T.emit("dv", lambda e, dc=dc, h=h, par=par: e.tensor_tensor(
                        qt_fm[par][:, dc * 256:(dc + 1) * 256],
                        qh_fm[par][:, dc * 256:(dc + 1) * 256],
                        cqb_sb[:, h * CH:(h + 1) * CH], ALU.mult),
                        waits=[w_cqb], incs=(("dv", 1),))
                    T.emit("dv", lambda e, dc=dc, h=h, c=c, par=par: e.tensor_scalar_mul(
                        qhat_stage[:, (2 * h + dc) * 256:(2 * h + dc + 1) * 256],
                        qt_fm[par][:, dc * 256:(dc + 1) * 256], ghat[h][c]),
                        incs=(("dv", 1),))
                qt_done = ("dv", T.cur("dv"))
                k_rot = rot(kh_fm[par], kh_raw[par], kev, "gp", 512)
                for dc in range(2):
                    T.emit("gp", lambda e, dc=dc, h=h, par=par: e.tensor_tensor(
                        kh_raw[par][:, dc * 256:(dc + 1) * 256],
                        kh_fm[par][:, dc * 256:(dc + 1) * 256],
                        ckb_sb[:, h * CH:(h + 1) * CH], ALU.mult),
                        waits=[w_ckb], incs=(("gp", 1),))
                khat_fm_done = ("gp", T.cur("gp"))
                prev_rot_k[par] = khat_fm_done

                kt_pe = T.emit("pe", lambda e, par=par: [
                    e.transpose(pt1[:, (jt * 2 + dc) * 128:(jt * 2 + dc + 1) * 128],
                                kh_raw[par][:, dc * 256 + jt * 128: dc * 256 + jt * 128 + 128],
                                ident_bf[:, :])
                    for jt in range(2) for dc in range(2)][-1],
                    waits=[khat_fm_done, w_id] + bgw("pt1"), incs=(("pe", 1),))[0]
                kt_ev = T.emit("dv", lambda e, par=par: e.tensor_copy(
                    khat_tm[par][:, :], pt1[:, 0:512]),
                    waits=[kt_pe], incs=(("dv", 1),))[0]
                bg["pt1"] = kt_ev

                at_pe = T.emit("pe", lambda e, par=par: [
                    e.matmul(ps[4][:, jt * 256:(jt + 1) * 256],
                             kh_fm[par][:, dc * 256 + jt * 128: dc * 256 + jt * 128 + 128],
                             qh_fm[par][:, dc * 256:(dc + 1) * 256],
                             start=(dc == 0), stop=(dc == 1))
                    for jt in range(2) for dc in range(2)][-1],
                    waits=[q_rot, k_rot] + bgw("ps4"), incs=(("pe", 1),))[0]
                atm_last = None
                for jt in range(2):
                    atm_last = T.emit("dv", lambda e, jt=jt, h=h, par=par: e.tensor_tensor(
                        attnT_sb[par][:, jt * 256:(jt + 1) * 256],
                        ps[4][:, jt * 256:(jt + 1) * 256],
                        dts_sb[:, (h * 2 + jt) * CH:(h * 2 + jt + 1) * CH], ALU.mult),
                        waits=[at_pe, w_dts], incs=(("dv", 1),))[0]
                bg["ps4"] = atm_last

                o_banks = (ps[5], ps[4])
                for it in range(2):
                    bank = o_banks[it]
                    bk = "ps5" if it == 0 else "ps4"
                    def fno(e, it=it, bank=bank, h=h, par=par):
                        for jt in range(2):
                            e.matmul(bank[:, :],
                                     attnT_sb[par][:, jt * 256 + it * 128: jt * 256 + it * 128 + 128],
                                     vh_tm[par][:, jt * DV:(jt + 1) * DV],
                                     start=(jt == 0), stop=False)
                        last = None
                        for dc in range(2):
                            last = e.matmul(bank[:, :],
                                     qt_fm[par][:, dc * 256 + it * 128: dc * 256 + it * 128 + 128],
                                     S_bf[:, (h * 2 + dc) * DV:(h * 2 + dc + 1) * DV],
                                     start=False, stop=(dc == 1))
                        return last
                    ope = T.emit("pe", fno,
                                 waits=[atm_last, qt_done, vev_toks[-1]] + bgw(bk) +
                                       (sinit if (c == 0 and h == 0) else []),
                                 incs=(("pe", 1),))[0]
                    oev = T.emit("dv", lambda e, it=it, bank=bank, h=h, c=c: e.tensor_copy(
                        o_stage[:, it * 2048 + h * DV: it * 2048 + (h + 1) * DV], bank[:, :]),
                        waits=[ope], incs=(("dv", 1),))[0]
                    bg[bk] = oev

                tu_banks = (ps[5], ps[4])
                tu_pe = T.emit("pe", lambda e, tb=tu_banks, par=par: [
                    e.matmul(tb[dc][:, :],
                             khat_tm[par][:, jt * 256 + dc * 128: jt * 256 + dc * 128 + 128],
                             vh_tm[par][:, jt * DV:(jt + 1) * DV],
                             start=(jt == 0), stop=(jt == 1))
                    for dc in range(2) for jt in range(2)][-1],
                    waits=[kt_ev] + bgw("ps5", "ps4"), incs=(("pe", 1),))[0]
                for dc in range(2):
                    tok_s = T.emit("dv", lambda e, dc=dc, h=h, tb=tu_banks: e.scalar_tensor_tensor(
                        S32[:, (h * 2 + dc) * DV:(h * 2 + dc + 1) * DV],
                        S32[:, (h * 2 + dc) * DV:(h * 2 + dc + 1) * DV], gC[h],
                        tb[dc][:, :], ALU.mult, ALU.add),
                        waits=[tu_pe], incs=(("dv", 1),))[0]
                    bg["ps5" if dc == 0 else "ps4"] = tok_s
                T.emit("dv", lambda e, h=h: e.tensor_copy(
                    S_bf[:, h * 2 * DV:(h + 1) * 2 * DV],
                    S32[:, h * 2 * DV:(h + 1) * 2 * DV]),
                    incs=(("dv", 1),))

            rot_toks = {}

            pre_pe = {}

            def chunk_pre(c):
                rot_war = [("dv", T.cur("dv")), ("gp", T.cur("gp"))]
                rot_toks[c] = (
                    dma("gld", "gp", cos_c[c % 2][:, :],
                        bass.AP(dcos, c * CH, [[TOK, 128], [1, CH]]), waits=rot_war),
                    dma("gld", "gp", sin_c[c % 2][:, :],
                        bass.AP(dsin, c * CH, [[TOK, 128], [1, CH]]), waits=rot_war),
                )
                for jt in range(2):
                    t = 2 * c + jt
                    guard = [pre_pe[(c - 1, jt)]] if (c - 1, jt) in pre_pe else []
                    a_tok = T.emit("ac", lambda e, t=t, jt=jt: e.activation(
                        xln2_tm[jt][:, :], x_sb[:, t * D:(t + 1) * D], AF.Identity,
                        bias=nmr_ln[:, NT + t:NT + t + 1], scale=r_ln[:, NT + t:NT + t + 1]),
                        waits=[ln2_r] + guard, incs=(("ac", 1),))[0]
                    pe_tok = T.emit("pe", lambda e, jt=jt: [
                        e.transpose(pt0[:, kc * 128:(kc + 1) * 128],
                                    xln2_tm[jt][:, kc * 128:(kc + 1) * 128],
                                    ident_bf[:, :]) for kc in range(KC)][-1],
                        waits=[a_tok, w_id] + bgw("pt0"), incs=(("pe", 1),))[0]
                    dv_tok = T.emit("dv", lambda e, jt=jt: e.tensor_copy(
                        bass.AP(xlnT2, jt * 128, [[KC * CH, 128], [CH, KC], [1, 128]]),
                        pt0[:, :]),
                        waits=[pe_tok], incs=(("dv", 1),))[0]
                    bg["pt0"] = dv_tok
                    pre_pe[(c, jt)] = pe_tok
                return ("dv", T.cur("dv"))

            pre_tok = chunk_pre(0)
            part1(0, 0, pre_tok)
            for c in range(NCH):
                for h in range(H):
                    if h + 1 < H:
                        part1(c, h + 1, pre_tok)
                    elif c + 1 < NCH:
                        pre_tok = chunk_pre(c + 1)
                        part1(c + 1, 0, pre_tok)
                    part2(c, h)

                ostage_done = ("dv", T.cur("dv"))
                t1 = dma("st", "sy",
                         bass.AP(o_hbm, c * CH * 2048, [[2048, 128], [128 * 2048, 2], [1, 2048]]),
                         bass.AP(o_stage, 0, [[2 * 2048, 128], [2048, 2], [1, 2048]]),
                         waits=[ostage_done])
                t2 = dma("st", "sy",
                         bass.AP(qhat_hbm, c * CH, [[TOK, 128], [128 * TOK, KC], [1, CH]]),
                         bass.AP(qhat_stage, 0, [[2048, 128], [CH, KC], [1, CH]]))
                T.pend["dv"].append(t1)
                T.pend["dv"].append(t2)
            s_final = ("dv", T.cur("dv"))
            sdma = dma("st", "sy",
                bass.AP(cc3_in, 0, [[DV, 128], [2 * 128 * DV, H], [128 * DV, 2], [1, DV]]),
                bass.AP(S32, 0, [[H * 2 * DV, 128], [2 * DV, H], [DV, 2], [1, DV]]),
                waits=[s_final])
            T.emit("gp", lambda e: e.collective_compute(
                "AllGather", ALU.bypass,
                replica_groups=[[0, 1], [2, 3], [4, 5], [6, 7]],
                ins=[cc3_in.ap().opt()], outs=[cc3_out.ap().opt()]),
                waits=[sdma], incs=(("cc", 1),))
            ag3_done = ("cc", T.cur("cc"))
            T.barrier()

    with ExitStack() as pb_stk:
        def bsb(name, cols, dtype=BF16):
            return pb_stk.enter_context(nc.sbuf_tensor(name, [128, cols], dtype))
        Wg_sb = bsb("Wg_sb", KC * 2048)
        Wo_sb = bsb("Wo_sb", 16 * D)
        S_seed = bsb("S_seed", H * 2 * DV)
        o_in = [bsb(f"o_in{p}", 2048) for p in range(4)]
        qhat_in = [bsb(f"qhat_in{p}", 2048) for p in range(3)]
        o_f32 = bsb("o_f32", 2048, F32)
        g_bf = [bsb(f"g_bf{p}", 2048) for p in range(4)]
        ogT = bsb("ogT", 16 * 128)
        xln_tmB = [bsb(f"xln_tmB{p}", D) for p in range(2)]
        xlnTB = bsb("xlnTB", D)

        _wg_tok = dma("gld", "gp", Wg_sb[:, :],
            bass.AP(dwg, 0, [[2048, 128], [128 * 2048, KC], [1, 2048]]))
        _wo_tok = dma("gld", "gp", Wo_sb[:, :],
            bass.AP(dwo, 0, [[D, 128], [128 * D, 16], [1, D]]))
        wgwo_tok = _wo_tok
        for vc in range(16):
            T.emit("dv", lambda e, vc=vc: e.tensor_scalar_mul(
                Wo_sb[:, vc * D:(vc + 1) * D], Wo_sb[:, vc * D:(vc + 1) * D],
                gnwt_sb[:, vc:vc + 1]),
                waits=[_wg_tok, _wo_tok, w_gnw], incs=(("dv", 1),))
        wo_ready = ("dv", T.cur("dv"))
        sc = dma("gld", "gp", S_seed[:, :],
                 bass.AP(cc3_out, 0, [[DV, 128], [2 * 128 * DV, H], [128 * DV, 2], [1, DV]]),
                 waits=[ag3_done])
        for i in range(4):
            T.emit("dv", lambda e, i=i: e.tensor_scalar_mul(
                S_seed[:, i * 2 * DV:(i + 1) * 2 * DV],
                S_seed[:, i * 2 * DV:(i + 1) * 2 * DV], flag_sb[:, 0:1]),
                waits=[sc, w_flag], incs=(("dv", 1),))
        seed_ready = ("dv", T.cur("dv"))

        b1_toks = {}
        b1_pe = {}
        qh_tok = None

        def phb1(t):
            nonlocal qh_tok
            cchunk = t // 2
            cpar = cchunk % 3
            par = t % 4
            if t % 2 == 0:
                qh_tok = dma("ld", "sy",
                             bass.AP(qhat_in[cpar], 0, [[2048, 128], [CH, KC], [1, CH]]),
                             bass.AP(qhat_hbm, cchunk * CH,
                                     [[TOK, 128], [128 * TOK, KC], [1, CH]]))
            oi_tok = dma("ld", "sy", o_in[par][:, :],
                         bass.AP(o_hbm, t * 128 * 2048, [[2048, 128], [1, 2048]]))
            guard = [b1_pe[t - 2]] if (t - 2) in b1_pe else []
            a_tok = T.emit("ac", lambda e, t=t, bpar=t % 2: e.activation(
                xln_tmB[bpar][:, :], x_sb[:, t * D:(t + 1) * D], AF.Identity,
                bias=nmr_ln[:, NT + t:NT + t + 1], scale=r_ln[:, NT + t:NT + t + 1]),
                waits=guard, incs=(("ac", 1),))[0]
            pe_tok = T.emit("pe", lambda e, bpar=t % 2: [
                e.transpose(pt0[:, kc * 128:(kc + 1) * 128],
                            xln_tmB[bpar][:, kc * 128:(kc + 1) * 128], ident_bf[:, :])
                for kc in range(KC)][-1],
                waits=[a_tok] + bgw("pt0"), incs=(("pe", 1),))[0]
            xT_tok = T.emit("dv", lambda e: e.tensor_copy(xlnTB[:, :], pt0[:, :]),
                            waits=[pe_tok], incs=(("dv", 1),))[0]
            bg["pt0"] = xT_tok
            b1_pe[t] = pe_tok
            g_toks = []
            for jb in range(4):
                bank = ps[2 + jb % 2]
                bk = f"ps{2 + jb % 2}"
                def fng(e, jb=jb, bank=bank):
                    last = None
                    for kc in range(KC):
                        last = e.matmul(bank[:, :], xlnTB[:, kc * 128:(kc + 1) * 128],
                                 Wg_sb[:, kc * 2048 + jb * 512: kc * 2048 + (jb + 1) * 512],
                                 start=(kc == 0), stop=(kc == KC - 1))
                    return last
                gpe = T.emit("pe", fng, waits=[xT_tok, _wg_tok] + bgw(bk),
                             incs=(("pe", 1),))[0]
                gev = T.emit("ac", lambda e, jb=jb, bank=bank, par=par: e.activation(
                    g_bf[par][:, jb * 512:(jb + 1) * 512], bank[:, :], AF.Silu,
                    bias=0.0, scale=1.0),
                    waits=[gpe], incs=(("ac", 1),))[0]
                g_toks.append(gev)
                bg[bk] = gev
            b1_toks[t] = (qh_tok, oi_tok, g_toks)

        def phb2(t):
            par = t % 4
            qh_t, oi_tok, g_toks = b1_toks.pop(t)
            oadd_last = None
            for h in range(H):
                bank = ps[4 + (h % 2)]
                bk = f"ps{4 + (h % 2)}"
                def fnc(e, h=h, t=t, bank=bank, cpar=(t // 2) % 3):
                    last = None
                    for dc in range(2):
                        last = e.matmul(bank[:, :],
                                 qhat_in[cpar][:, (2 * h + dc) * CH + (t % 2) * 128:
                                         (2 * h + dc) * CH + (t % 2) * 128 + 128],
                                 S_seed[:, (h * 2 + dc) * DV:(h * 2 + dc + 1) * DV],
                                 start=(dc == 0), stop=(dc == 1))
                    return last
                cpe = T.emit("pe", fnc,
                             waits=[seed_ready, qh_t] + bgw(bk),
                             incs=(("pe", 1),))[0]
                oadd_last = T.emit("dv", lambda e, h=h, bank=bank, par=par: e.scalar_tensor_tensor(
                    o_f32[:, h * DV:(h + 1) * DV], bank[:, :], 1.0,
                    o_in[par][:, h * DV:(h + 1) * DV], ALU.mult, ALU.add),
                    waits=[cpe, oi_tok], incs=(("dv", 1),))[0]
                bg[bk] = oadd_last
                T.emit("ac", lambda e, h=h: e.activation(
                    junk[:, 0:DV], o_f32[:, h * DV:(h + 1) * DV], AF.Square,
                    bias=0.0, scale=1.0, accum_out=ssq[:, h:h + 1]),
                    waits=[oadd_last], incs=(("ac", 1),))
            sq_done = ("ac", T.cur("ac"))
            T.emit("dv", lambda e: e.tensor_scalar(ssq[:, 4:8], ssq[:, 0:4], 1.0 / DV, EPS,
                                                   ALU.mult, ALU.add),
                   waits=[sq_done], incs=(("dv", 1),))
            T.emit("dv", lambda e: e.reciprocal(ssq[:, 0:4], ssq[:, 4:8]), incs=(("dv", 1),))
            rcp = ("dv", T.cur("dv"))
            T.emit("ac", lambda e: e.activation(rh_sb[:, 0:4], ssq[:, 0:4], AF.Sqrt,
                                                bias=0.0, scale=1.0),
                   waits=[rcp], incs=(("ac", 1),))
            r0t = ("ac", T.cur("ac"))
            T.emit("dv", lambda e: e.tensor_tensor(ssq[:, 0:4], rh_sb[:, 0:4], rh_sb[:, 0:4],
                                                   ALU.mult),
                   waits=[r0t], incs=(("dv", 1),))
            T.emit("dv", lambda e: e.tensor_tensor(ssq[:, 0:4], ssq[:, 0:4], ssq[:, 4:8],
                                                   ALU.mult), incs=(("dv", 1),))
            T.emit("dv", lambda e: e.tensor_scalar(ssq[:, 0:4], ssq[:, 0:4], -0.5, 1.5,
                                                   ALU.mult, ALU.add), incs=(("dv", 1),))
            T.emit("dv", lambda e: e.tensor_tensor(rh_sb[:, 0:4], rh_sb[:, 0:4], ssq[:, 0:4],
                                                   ALU.mult), incs=(("dv", 1),))
            rh_done = ("dv", T.cur("dv"))
            T.pend["ac"].append(rh_done)
            og_last = None
            for h in range(H):
                og_last = T.emit("dv", lambda e, h=h, par=par: e.scalar_tensor_tensor(
                    g_bf[par][:, h * DV:(h + 1) * DV], o_f32[:, h * DV:(h + 1) * DV],
                    rh_sb[:, 0 + h:1 + h], g_bf[par][:, h * DV:(h + 1) * DV],
                    ALU.mult, ALU.mult),
                    waits=[rh_done, g_toks[h]], incs=(("dv", 1),))[0]
            og_last_gp = og_last
            t_last = None
            for half in range(2):
                ppe = T.emit("pe", lambda e, half=half, par=par: [
                    e.transpose(pt1[:, i * 128:(i + 1) * 128],
                                g_bf[par][:, half * 1024 + i * 128: half * 1024 + (i + 1) * 128],
                                ident_bf[:, :]) for i in range(8)][-1],
                    waits=[og_last, og_last_gp] + bgw("pt1"), incs=(("pe", 1),))[0]
                t_last = T.emit("dv", lambda e, half=half: e.tensor_copy(
                    ogT[:, half * 1024:(half + 1) * 1024], pt1[:, :]),
                    waits=[ppe], incs=(("dv", 1),))[0]
                bg["pt1"] = t_last
            for jb in range(2):
                bank = ps[jb]
                def fny(e, jb=jb, bank=bank):
                    last = None
                    for vc in range(16):
                        last = e.matmul(bank[:, :], ogT[:, vc * 128:(vc + 1) * 128],
                                 Wo_sb[:, vc * D + jb * 512: vc * D + (jb + 1) * 512],
                                 start=(vc == 0), stop=(vc == 15))
                    return last
                ype = T.emit("pe", fny, waits=[t_last, wo_ready] + bgw(f"ps{jb}"),
                             incs=(("pe", 1),))[0]
                yev = T.emit("dv", lambda e, t=t, jb=jb, bank=bank: e.scalar_tensor_tensor(
                    x_sb[:, t * D + jb * 512: t * D + (jb + 1) * 512], bank[:, :], 1.0,
                    x_sb[:, t * D + jb * 512: t * D + (jb + 1) * 512], ALU.mult, ALU.add),
                    waits=[ype], incs=(("dv", 1),))[0]
                bg[f"ps{jb}"] = yev
            ln_stats(t, scratch=ogT)
            T.pend["sy"].append(("dv", T.cur("dv")))
            T.pend["sy"].append(("pe", T.cur("pe")))

        phb1(0)
        phb1(1)
        phb1(2)
        for t in range(NT):
            if t + 3 < NT:
                phb1(t + 3)
            phb2(t)
        T.barrier()

    ln3_r = rsqrt_chain(2, D)

    with ExitStack() as ffn2_stk:
        Mh2 = ffn2_stk.enter_context(nc.sbuf_tensor("Mh2", [128, KC * D], BF16))
        Ml2 = ffn2_stk.enter_context(nc.sbuf_tensor("Ml2", [128, KC * D], BF16))
        xln_tm2 = [ffn2_stk.enter_context(nc.sbuf_tensor(f"xln_tm2{p}", [128, D], BF16))
                   for p in range(2)]
        xlnT2b = [ffn2_stk.enter_context(nc.sbuf_tensor(f"xlnT2b{p}", [128, D], BF16))
                  for p in range(2)]
        brow2 = ffn2_stk.enter_context(nc.sbuf_tensor("brow2", [128, D], BF16))
        brow2_tok = dma("gld", "gp", brow2[0:1, :], dbias[:, :])
        m2_ld = []
        for i, dst in enumerate((Mh2, Ml2)):
            for r in range(NCORES):
                m2_ld.append(dma("ld", "sy",
                    bass.AP(dst, 128 * r, [[KC * D, 128], [D, KC], [1, 128]]),
                    bass.AP(cc2_out, (r * 2 + i) * D * 128,
                            [[128, 128], [128 * 128, KC], [1, 128]])))
        ffn_pass(2, Mh2, Ml2, m2_ld, xln_tm2, xlnT2b, ln3_r, do_stats=False, do_out=True,
                 brow=brow2, brow_tok=brow2_tok)

    T.emit("sy", lambda e: e.nop(), waits=list(out_toks), incs=(("gp", 1),))

    with nc.Block() as block:
        T.realize(block, semh)
    stk.close()
    return nc


_NC_CACHE = None


def _get_nc():
    global _NC_CACHE
    if _NC_CACHE is None:
        _NC_CACHE = build_nc()
    return _NC_CACHE


def kernel(L0_attn_x, L0_attn_Wq, L0_attn_Wk, L0_attn_Wv, L0_attn_Wg, L0_attn_Wo,
           L0_attn_gn_weight, L0_ffn_w_freq_real, L0_ffn_w_freq_imag, L0_ffn_bias):
    c = host_constants()
    x = np.ascontiguousarray(np.asarray(L0_attn_x, dtype=np.float32))
    f32 = lambda a: np.ascontiguousarray(np.asarray(a, dtype=np.float32))
    gnw = np.asarray(L0_attn_gn_weight, dtype=np.float32)
    vc_idx = (np.arange(16)[None, :] * 128 + np.arange(128)[:, None]) % 512
    gnwt = np.ascontiguousarray(gnw[vc_idx].astype(np.float32))
    shared = {
        "wq": f32(L0_attn_Wq), "wk": f32(L0_attn_Wk), "wv": f32(L0_attn_Wv),
        "wg": f32(L0_attn_Wg), "wo": f32(L0_attn_Wo),
        "wfr": f32(L0_ffn_w_freq_real), "wfi": f32(L0_ffn_w_freq_imag),
        "bias": f32(L0_ffn_bias).reshape(1, D),
        "gnwt": gnwt,
        "dts": c["dts"].astype(np.float32),
        "cqb": c["cqb"].astype(np.float32),
        "ckb": c["ckb"].astype(np.float32),
        "identc": c["ident"].astype(np.float32),
    }
    in_maps = []
    for core in range(NCORES):
        b, half = core // 2, core % 2
        pos = slice(half * TOK, (half + 1) * TOK)
        m = dict(shared)
        m["x"] = np.ascontiguousarray(x[b, pos, :])
        m["frs"] = np.ascontiguousarray(c["C"][:, core * 128:(core + 1) * 128].astype(np.float32))
        m["fis"] = np.ascontiguousarray(c["S"][:, core * 128:(core + 1) * 128].astype(np.float32))
        m["fisn"] = np.ascontiguousarray((-c["S"][:, core * 128:(core + 1) * 128]).astype(np.float32))
        m["cosf"] = np.ascontiguousarray(c["cos"][pos, :].T.astype(np.float32))
        m["sinf"] = np.ascontiguousarray(c["sin"][pos, :].T.astype(np.float32))
        m["flagcol"] = np.full((128, 1), float(half), dtype=np.float32)
        in_maps.append(m)
    nc = _get_nc()
    res = run_bass_kernel_spmd(nc, in_maps, core_ids=list(range(NCORES)))
    out = np.empty((4, 4096, D), dtype=np.float32)
    for core in range(NCORES):
        b, half = core // 2, core % 2
        out[b, half * TOK:(half + 1) * TOK, :] = res.results[core]["y"]
    return out
